# version 05 (final)
# baseline (speedup 1.0000x reference)
import numpy as np
import ml_dtypes
from collections import defaultdict
from contextlib import ExitStack

import concourse.bass as bass
import concourse.mybir as mybir
from concourse.bass_utils import run_bass_kernel_spmd

F32 = mybir.dt.float32
BF16 = mybir.dt.bfloat16
I32 = mybir.dt.int32
AF = mybir.ActivationFunctionType
ALU = mybir.AluOpType
AX = mybir.AxisListType.X
BF = ml_dtypes.bfloat16

D_MODEL = 1024
TTOT = 8192
B = 2

ENGNAMES = ("tensor", "vector", "scalar", "gpsimd", "sync")


class Prog:

    def __init__(self, name="prog"):
        self.nc = bass.Bass("TRN2", target_bir_lowering=False)
        self.es = ExitStack()
        self.cnt = defaultdict(int)
        self.seen = defaultdict(int)
        self.sems = {}
        for d in ("te", "ve", "se", "pe", "sp", "dsp", "dse", "dpe"):
            self.sems[d] = self.es.enter_context(self.nc.semaphore("sem_" + d))
        self.block = self.es.enter_context(self.nc.Block())
        self._names = set()
        self._cc = {}
        self._cc_tile = None

    def _nm(self, base):
        i = 0
        nm = base
        while nm in self._names:
            i += 1
            nm = f"{base}_{i}"
        self._names.add(nm)
        return nm

    def sbuf(self, base, shape, dt=F32):
        return self.es.enter_context(self.nc.sbuf_tensor(self._nm(base), shape, dt))

    def psum(self, base, shape, dt=F32):
        return self.es.enter_context(self.nc.psum_tensor(self._nm(base), shape, dt))

    def din(self, name, shape, dt=F32):
        self._names.add(name)
        return self.nc.dram_tensor(name, list(shape), dt, kind="ExternalInput").ap()

    def dout(self, name, shape, dt=F32):
        self._names.add(name)
        return self.nc.dram_tensor(name, list(shape), dt, kind="ExternalOutput").ap()

    def dint(self, name, shape, dt=F32, shared=False):
        return self.nc.dram_tensor(self._nm(name), list(shape), dt,
                                   addr_space=("Shared" if shared else "Local")).ap()

    def ccol(self, val):
        val = float(val)
        if self._cc_tile is None:
            self._cc_tile = self.es.enter_context(
                self.nc.sbuf_tensor(self._nm("cconst"), [128, 32], F32))
        if val not in self._cc:
            j = len(self._cc)
            assert j < 32
            self._cc[val] = j
            pr = self

            def f(e):
                e.tick(e.g.memset(pr._cc_tile[:, j:j + 1], val))
            self.sec("vector", f)
        j = self._cc[val]
        return self._cc_tile[:, j:j + 1]

    def sec(self, engname, fn, auto=True):
        pr = self

        def wrapper(eng):
            e = E(pr, eng, engname)
            if auto:
                e.waitall("te", "ve", "se", "pe", "sp", "dsp", "dse", "dpe")
            fn(e)
        getattr(self.block, engname)(wrapper)

    def close(self):
        self.es.close()
        return self.nc


_ENG2DOM = {"tensor": "te", "vector": "ve", "scalar": "se", "gpsimd": "pe", "sync": "sp"}
_ENG2DMA = {"sync": "dsp", "scalar": "dse", "gpsimd": "dpe", "vector": "dve_dma", "tensor": None}


class E:
    def __init__(self, pr, eng, name):
        self.pr = pr
        self.g = eng
        self.name = name
        self.dom = _ENG2DOM[name]

    def need(self, dom, cnt=None):
        if cnt is None:
            cnt = self.pr.cnt[dom]
        if cnt <= 0:
            return
        key = (self.name, dom)
        if self.pr.seen[key] < cnt:
            self.g.wait_ge(self.pr.sems[dom], cnt)
            self.pr.seen[key] = cnt

    def tick(self, instr, dom=None, amt=1):
        dom = dom or self.dom
        instr.then_inc(self.pr.sems[dom], amt)
        self.pr.cnt[dom] += amt
        if dom not in ("dsp", "dse", "dpe", "dve_dma"):
            self.pr.seen[(self.name, dom)] = self.pr.cnt[dom]
        return self.pr.cnt[dom]

    def fence(self, ins):
        c = self.tick(ins)
        self.g.wait_ge(self.pr.sems[self.dom], c)
        return c

    def barrier(self):
        c = self.pr.cnt[self.dom]
        if c > 0:
            self.g.wait_ge(self.pr.sems[self.dom], c)

    def dma(self, out, in_, **kw):
        d = _ENG2DMA[self.name]
        ins = self.g.dma_start(out=out, in_=in_, **kw)
        return self.tick(ins, d, 16)

    def waitall(self, *doms):
        for d in doms:
            self.need(d)



def _bf(x):
    return np.ascontiguousarray(x).astype(BF)


def prep_inputs(inp):
    p = {}
    x = np.asarray(inp["L0_attn_a_hidden_states"])
    p["x_fm"] = np.ascontiguousarray(x.transpose(0, 2, 1))

    p["w_inproj"] = np.ascontiguousarray(np.asarray(inp["L0_attn_a_in_proj_w"]).T)
    p["l0_conv_w"] = np.asarray(inp["L0_attn_a_conv_w"])
    p["l0_conv_b"] = np.asarray(inp["L0_attn_a_conv_b"])
    p["l0_dt_bias"] = np.asarray(inp["L0_attn_a_dt_bias"])
    p["l0_A_log"] = np.asarray(inp["L0_attn_a_A_log"])
    p["l0_D"] = np.asarray(inp["L0_attn_a_D"])
    p["l0_norm_w"] = np.asarray(inp["L0_attn_a_norm_w"])
    p["l0_ssm_norm_w"] = np.asarray(inp["L0_attn_a_ssm_norm_w"])
    p["w_outproj"] = np.ascontiguousarray(np.asarray(inp["L0_attn_a_out_proj_w"]).T)
    p["l0_post_norm_w"] = np.asarray(inp["L0_attn_a_post_norm_w"])
    p["w_gate"] = np.ascontiguousarray(np.asarray(inp["L0_attn_a_gate_w"]).T)
    p["w_up"] = np.ascontiguousarray(np.asarray(inp["L0_attn_a_up_w"]).T)
    p["w_down"] = np.ascontiguousarray(np.asarray(inp["L0_attn_a_down_w"]).T)

    p["m15_router"] = np.asarray(inp["L0_ffn_router_W"])
    p["m15_router_b"] = np.asarray(inp["L0_ffn_router_b"])
    p["m15_sW1"] = np.asarray(inp["L0_ffn_sW1"])
    p["m15_sb1"] = np.asarray(inp["L0_ffn_sb1"])
    p["m15_sW2"] = np.asarray(inp["L0_ffn_sW2"])
    p["m15_sb2"] = np.asarray(inp["L0_ffn_sb2"])
    p["m15_eW1"] = np.ascontiguousarray(np.asarray(inp["L0_ffn_eW1"]))
    p["m15_eb1"] = np.asarray(inp["L0_ffn_eb1"])
    p["m15_eW2"] = np.ascontiguousarray(np.asarray(inp["L0_ffn_eW2"]))
    p["m15_eb2"] = np.asarray(inp["L0_ffn_eb2"])

    p["s4_lAr"] = np.asarray(inp["L0_attn_b_log_A_real"])
    p["s4_Ai"] = np.asarray(inp["L0_attn_b_A_imag"])
    p["s4_BT"] = np.ascontiguousarray(np.asarray(inp["L0_attn_b_B"]).T)
    C = np.asarray(inp["L0_attn_b_C"])
    p["s4_CrT"] = np.ascontiguousarray(C[:, :, 0].T)
    p["s4_CiT"] = np.ascontiguousarray(C[:, :, 1].T)
    p["s4_D"] = np.asarray(inp["L0_attn_b_D"])
    p["s4_log_step"] = np.asarray(inp["L0_attn_b_log_step"])

    p["m1_Win"] = np.ascontiguousarray(np.asarray(inp["L1_attn_a_W_in"]).transpose(0, 2, 1))
    p["m1_conv_w"] = np.asarray(inp["L1_attn_a_conv_w"])
    p["m1_conv_b"] = np.asarray(inp["L1_attn_a_conv_b"])
    p["m1_Wx"] = np.ascontiguousarray(np.asarray(inp["L1_attn_a_W_x"]).transpose(0, 2, 1))
    p["m1_Wdt"] = np.ascontiguousarray(np.asarray(inp["L1_attn_a_W_dt"]).transpose(0, 2, 1))
    p["m1_bdt"] = np.asarray(inp["L1_attn_a_b_dt"])
    p["m1_Alog"] = np.ascontiguousarray(np.asarray(inp["L1_attn_a_A_log"]))
    p["m1_D"] = np.asarray(inp["L1_attn_a_D_par"])
    p["m1_Wout"] = np.ascontiguousarray(np.asarray(inp["L1_attn_a_W_out"]).transpose(0, 2, 1))

    params = np.asarray(inp["L1_attn_b_params"])
    L = np.zeros((16, 64, 64), np.float32)
    r, c = np.tril_indices(64)
    L[:, r, c] = params
    p["att_L"] = L
    p["att_Wv"] = np.ascontiguousarray(np.asarray(inp["L1_attn_b_Wv"]).T)
    p["att_bv"] = np.asarray(inp["L1_attn_b_bv"])
    p["att_Wo"] = np.ascontiguousarray(np.asarray(inp["L1_attn_b_Wo"]).T)
    p["att_bo"] = np.asarray(inp["L1_attn_b_bo"])

    p["m32_router"] = np.ascontiguousarray(np.asarray(inp["L1_ffn_router_w"]).T)
    p["m32_router_b"] = np.asarray(inp["L1_ffn_router_bias"])
    p["m32_gate"] = np.ascontiguousarray(np.asarray(inp["L1_ffn_gate_w"]).transpose(0, 2, 1))
    p["m32_up"] = np.ascontiguousarray(np.asarray(inp["L1_ffn_up_w"]).transpose(0, 2, 1))
    p["m32_down"] = np.ascontiguousarray(np.asarray(inp["L1_ffn_down_w"]).transpose(0, 2, 1))
    p["m32_sgate"] = np.ascontiguousarray(np.asarray(inp["L1_ffn_sgate_w"]).T)
    p["m32_sup"] = np.ascontiguousarray(np.asarray(inp["L1_ffn_sup_w"]).T)
    p["m32_sdown"] = np.ascontiguousarray(np.asarray(inp["L1_ffn_sdown_w"]).T)

    p["ones_bf"] = np.ones((128, 128), BF)
    p["ident_bf"] = np.eye(128).astype(BF)
    p["ident_f32"] = np.eye(128).astype(np.float32)
    m = np.zeros((128, 128), np.float32)
    s_idx, t_idx = np.meshgrid(np.arange(128), np.arange(128), indexing="ij")
    m[s_idx > t_idx] = 1e30
    p["maskbig"] = m
    for Ecnt in (15, 16, 32):
        bc = np.zeros((Ecnt, Ecnt * 128), np.float32)
        for e_ in range(Ecnt):
            bc[e_, e_ * 128:(e_ + 1) * 128] = 1.0
        p[f"bcast{Ecnt}"] = bc.astype(BF)
    s64 = np.zeros((64, 64 * 128), np.float32)
    for pp in range(64):
        s64[pp, pp * 128:(pp + 1) * 128] = 1.0
    p["sel64"] = s64.astype(BF)
    e64 = np.zeros((128, 64 * 64), np.float32)
    for pp in range(64):
        e64[:, pp * 64 + pp] = 1.0
    p["ecol64"] = e64.astype(BF)
    p["l0_D2048"] = np.repeat(np.asarray(inp["L0_attn_a_D"]), 64)
    return p



def cdiv(a, b):
    return (a + b - 1) // b


class Banks:

    def __init__(self, pr, n=6, name="pb"):
        self.pr = pr
        self.tiles = [pr.psum(name, [128, 512], F32) for _ in range(n)]
        self.dep = [None] * n
        self.i = 0

    def get(self, e):
        b = self.i % len(self.tiles)
        self.i += 1
        if self.dep[b] is not None:
            e.need(*self.dep[b])
        return b, self.tiles[b]

    def release(self, b, dom, cnt):
        self.dep[b] = (dom, cnt)


def wload(pr, e, dram, K, M, name="w"):
    nk = cdiv(K, 128)
    sb = pr.sbuf(name, [128, nk * M], BF16)
    for k in range(nk):
        kr = min(128, K - k * 128)
        e.dma(sb[0:kr, k * M:k * M + M], dram[k * 128:k * 128 + kr, :])
    return sb, nk


def vcols(pr, e, dram_vec, N, name="vc"):
    nk = cdiv(N, 128)
    sb = pr.sbuf(name, [128, nk], F32)
    v2 = dram_vec.rearrange("(k p) -> k p", p=128) if N % 128 == 0 else None
    for k in range(nk):
        kr = min(128, N - k * 128)
        if v2 is not None:
            e.dma(sb[0:kr, k:k + 1], v2[k].rearrange("(p a) -> p a", a=1))
        else:
            e.dma(sb[0:kr, k:k + 1],
                  dram_vec[k * 128:k * 128 + kr].rearrange("(p a) -> p a", a=1))
    return sb


def mm(e, pout, wsb, M, m0, mlen, xsb, T, t0, tlen, ks, acc=False, stop=True):
    n = len(ks)
    ins = None
    for i, (kw, kx, kr) in enumerate(ks):
        ins = e.g.matmul(pout[0:mlen, 0:tlen],
                         wsb[0:kr, kw * M + m0: kw * M + m0 + mlen],
                         xsb[0:kr, kx * T + t0: kx * T + t0 + tlen],
                         start=(i == 0 and not acc), stop=(stop and i == n - 1))
    return ins


def emit_cast_bf(pr, x_sb, nf, T, name="xb"):
    xb = pr.sbuf(name, [128, nf * T], BF16)

    def f(e):
        e.waitall("ve", "te", "se", "dsp", "dpe")
        e.tick(e.g.tensor_copy(xb[:, :], x_sb[:, 0:nf * T]))
    pr.sec("gpsimd", f)
    return xb


S4STOP = 99
LNSTOP = None


def emit_ln_apply(pr, x_sb, nf, T, out_bf, mrow, rs, mode="ln", gain_cols=None):
    def fapply(e):
        e.waitall("ve", "se")
        for kj in range(nf):
            xch = x_sb[:, kj * T:(kj + 1) * T]
            och = out_bf[:, kj * T:(kj + 1) * T]
            gc = gain_cols[:, kj:kj + 1] if gain_cols is not None else None
            if mode == "rms":
                if gc is not None:
                    last = e.g.scalar_tensor_tensor(och, xch, gc, rs[:, 0:T], ALU.mult, ALU.mult)
                else:
                    last = e.g.tensor_mul(och, xch, rs[:, 0:T])
            else:
                e.fence(e.g.tensor_sub(och, xch, mrow[:, 0:T]))
                if gc is not None:
                    last = e.g.scalar_tensor_tensor(och, och, gc, rs[:, 0:T], ALU.mult, ALU.mult)
                else:
                    last = e.g.tensor_mul(och, och, rs[:, 0:T])
            if kj == nf - 1:
                e.tick(last)
    pr.sec("vector", fapply)


def emit_ln(pr, banks, x_sb, nf, T, ones_bf, out_bf, mode="ln",
            gain_cols=None, eps=1e-5, x_bf=None, sq_bf=None, name="ln",
            apply=True, stats_bufs=None):
    if x_bf is None:
        x_bf = emit_cast_bf(pr, x_sb, nf, T, name=name + "_xb")
    if sq_bf is None:
        sq_bf = pr.sbuf(name + "_sq", [128, nf * T], BF16)

    def fsq(e):
        e.waitall("pe")
        e.tick(e.g.tensor_mul(sq_bf[:, :], x_bf[:, 0:nf * T], x_bf[:, 0:nf * T]))
    pr.sec("gpsimd", fsq)

    if LNSTOP == "sq":
        return None, None
    if stats_bufs is not None:
        mrow, rs = stats_bufs
    elif mode == "rms":
        mrow = None
        rs = pr.sbuf(name + "_rs", [128, T], F32)
    else:
        mrow = pr.sbuf(name + "_m", [128, T + 512], F32)
        rs = pr.sbuf(name + "_rs", [128, T], F32)
    tmp = mrow[:, T:T + 512] if mrow is not None else None
    nt = cdiv(T, 512)
    inv = 1.0 / (nf * 128)
    for tb in range(nt):
        t0 = tb * 512
        tl = min(512, T - t0)
        holder = {}

        def ftens(e, t0=t0, tl=tl, holder=holder):
            e.waitall("pe", "dsp", "dpe")
            bs = []
            if mode != "rms":
                b1, p1 = banks.get(e)
                for kj in range(nf):
                    e.g.matmul(p1[:, 0:tl], ones_bf[:, :],
                               x_bf[:, kj * T + t0:kj * T + t0 + tl],
                               start=(kj == 0), stop=(kj == nf - 1))
                bs.append((b1, p1))
            b2, p2 = banks.get(e)
            ins = None
            for kj in range(nf):
                ins = e.g.matmul(p2[:, 0:tl], ones_bf[:, :],
                                 sq_bf[:, kj * T + t0:kj * T + t0 + tl],
                                 start=(kj == 0), stop=(kj == nf - 1))
            bs.append((b2, p2))
            e.tick(ins)
            holder["bs"] = bs
        pr.sec("tensor", ftens)

        def fvec(e, t0=t0, tl=tl, holder=holder):
            e.waitall("te")
            bs = holder["bs"]
            if mode != "rms":
                (b1, p1), (b2, p2) = bs
                e.fence(e.g.tensor_scalar_mul(mrow[:, t0:t0 + tl], p1[:, 0:tl], inv))
                e.fence(e.g.tensor_mul(tmp[:, 0:tl], mrow[:, t0:t0 + tl], mrow[:, t0:t0 + tl]))
                ins = e.g.scalar_tensor_tensor(rs[:, t0:t0 + tl], p2[:, 0:tl], inv,
                                               tmp[:, 0:tl], ALU.mult, ALU.subtract)
            else:
                (b2, p2), = bs
                ins = e.g.tensor_scalar_mul(rs[:, t0:t0 + tl], p2[:, 0:tl], inv)
            c = e.tick(ins)
            for (b, _p) in bs:
                banks.release(b, "ve", c)
        pr.sec("vector", fvec)

    if LNSTOP == "stats":
        return mrow, rs
    if mode == "ln":
        scl, bias = 1.0, eps
    elif mode == "lnrms":
        scl, bias = 1.0 + 1e-6, 1e-11
    else:
        scl, bias = 1.0, eps

    bias_ap = pr.ccol(bias)

    def fact(e):
        e.waitall("ve")
        e.tick(e.g.activation(rs[:, 0:T], rs[:, 0:T], AF.Sqrt, bias=bias_ap, scale=scl))
    pr.sec("scalar", fact)

    if LNSTOP == "sqrt":
        return mrow, rs

    def frecip(e):
        e.waitall("se")
        e.fence(e.g.reciprocal(rs[:, 0:T], rs[:, 0:T]))
    pr.sec("vector", frecip)

    if LNSTOP == "recip":
        return mrow, rs
    if not apply:
        return mrow, rs

    def fapply(e):
        e.waitall("ve")
        for kj in range(nf):
            xch = x_sb[:, kj * T:(kj + 1) * T]
            och = out_bf[:, kj * T:(kj + 1) * T]
            gc = gain_cols[:, kj:kj + 1] if gain_cols is not None else None
            if mode == "rms":
                if gc is not None:
                    last = e.g.scalar_tensor_tensor(och, xch, gc, rs[:, 0:T], ALU.mult, ALU.mult)
                else:
                    last = e.g.tensor_mul(och, xch, rs[:, 0:T])
            else:
                e.g.tensor_sub(och, xch, mrow[:, 0:T])
                if gc is not None:
                    last = e.g.scalar_tensor_tensor(och, och, gc, rs[:, 0:T], ALU.mult, ALU.mult)
                else:
                    last = e.g.tensor_mul(och, och, rs[:, 0:T])
            if kj == nf - 1:
                e.tick(last)
    pr.sec("vector", fapply)
    return mrow, rs



def build_moe15(T=2048, stop_after=None, dbg=False):
    pr = Prog("moe15")
    nf, E_, H, K_, nh = 8, 15, 1024, 3, 8
    x_d = pr.din("x", [D_MODEL, T])
    ones_d = pr.din("ones_bf", [128, 128], BF16)
    identbf_d = pr.din("ident_bf", [128, 128], BF16)
    bc15_d = pr.din("bcast15", [15, 15 * 128], BF16)
    idf_d = pr.din("ident_f32", [128, 128])
    rW_d = pr.din("rW", [D_MODEL, E_])
    rb_d = pr.din("rb", [E_])
    sW1_d = pr.din("sW1", [D_MODEL, H])
    sb1_d = pr.din("sb1", [H])
    sW2_d = pr.din("sW2", [H, D_MODEL])
    sb2_d = pr.din("sb2", [D_MODEL])
    eW1_d = pr.din("eW1", [E_, D_MODEL, H])
    eb1_d = pr.din("eb1", [E_, H])
    eW2_d = pr.din("eW2", [E_, H, D_MODEL])
    eb2_d = pr.din("eb2", [E_, D_MODEL])
    xo_d = pr.dout("xo", [D_MODEL, T])

    banks = Banks(pr, 6)
    x_sb = pr.sbuf("x", [128, nf * T])
    xt_bf = pr.sbuf("xt", [128, nf * T], BF16)
    scr_bf = pr.sbuf("scr", [128, nf * T], BF16)
    wbuf = [pr.sbuf("wb", [128, 8 * 1024], BF16) for _ in range(2)]
    wdep = [None, None]
    ones_bf = pr.sbuf("ones", [128, 128], BF16)
    ident_f = pr.sbuf("idf", [128, 128])
    rWf_sb = pr.sbuf("rWf", [128, nf * 16])
    rWhi = pr.sbuf("rWhi", [128, nf * 16], BF16)
    rWlo = pr.sbuf("rWlo", [128, nf * 16], BF16)
    ident_bf = pr.sbuf("identbf", [128, 128], BF16)
    bc15_sb = pr.sbuf("bc15", [15, 15 * 128], BF16)
    eb2_sb = pr.sbuf("eb2sb", [15, 1024], BF16)
    sb1_c = pr.sbuf("sb1c", [128, nh])
    sb2_c = pr.sbuf("sb2c", [128, nf])
    eb1_c = pr.sbuf("eb1c", [128, E_ * nh])
    cwT_bf = pr.sbuf("cwTb", [16, T], BF16)
    cwb_bf = pr.sbuf("cwb", [128, T], BF16)
    lg = pr.sbuf("lg", [128, 16 * 2])
    pr._dbg_lgraw = pr.sbuf("dbglgraw", [128, 32])
    cwt_b = pr.sbuf("cwtb16", [128, 16], BF16)
    mx = pr.sbuf("mx", [128, 12])
    cw_tile = pr.sbuf("cwt", [128, 16])

    def fldsp(e):
        for kj in range(nf):
            e.dma(x_sb[:, kj * T:(kj + 1) * T], x_d[kj * 128:(kj + 1) * 128, :])
        e.dma(ones_bf[:, :], ones_d)
        e.dma(ident_f[:, :], idf_d)
        e.dma(ident_bf[:, :], identbf_d)
        e.dma(bc15_sb[0:15, :], bc15_d)
    pr.sec("sync", fldsp)

    def fldpe(e):
        for kj in range(nf):
            e.dma(rWf_sb[:, kj * 16:kj * 16 + E_], rW_d[kj * 128:(kj + 1) * 128, :])
        e.dma(eb2_sb[0:E_, :], eb2_d)
        for kj in range(nh):
            e.dma(sb1_c[:, kj:kj + 1], sb1_d.rearrange("(k p) -> k p", p=128)[kj].rearrange("(p a) -> p a", a=1))
        for kj in range(nf):
            e.dma(sb2_c[:, kj:kj + 1], sb2_d.rearrange("(k p) -> k p", p=128)[kj].rearrange("(p a) -> p a", a=1))
        for ei in range(E_):
            for kj in range(nh):
                e.dma(eb1_c[:, ei * nh + kj:ei * nh + kj + 1],
                      eb1_d[ei].rearrange("(k p) -> k p", p=128)[kj].rearrange("(p a) -> p a", a=1))
        for kj in range(nf):
            e.dma(wbuf[0][:, kj * 1024:(kj + 1) * 1024], sW1_d[kj * 128:(kj + 1) * 128, :])
        for kj in range(nh):
            e.dma(wbuf[1][:, kj * 1024:(kj + 1) * 1024], sW2_d[kj * 128:(kj + 1) * 128, :])
    pr.sec("gpsimd", fldpe)

    def fmemset(e):
        e.tick(e.g.memset(lg[:, :], 0.0))
    pr.sec("vector", fmemset)

    if stop_after == "loads":
        def fstL(e):
            e.waitall("ve", "se", "te", "pe", "dpe")
            for kj in range(nf):
                e.dma(xo_d[kj * 128:(kj + 1) * 128, :], x_sb[:, kj * T:(kj + 1) * T])
            e.need("dsp")
        pr.sec("sync", fstL)
        return pr

    def fcast(e):
        e.waitall("ve", "te", "se", "dsp", "dpe")
        e.tick(e.g.tensor_copy(xt_bf[:, :], x_sb[:, 0:nf * T]))
    pr.sec("gpsimd", fcast)
    if stop_after == "cast":
        def fstC(e):
            e.waitall("ve", "se", "te", "pe", "dpe")
            for kj in range(nf):
                e.dma(xo_d[kj * 128:(kj + 1) * 128, :], x_sb[:, kj * T:(kj + 1) * T])
            e.need("dsp")
        pr.sec("sync", fstC)
        return pr
    pr._moe_ln_stats = emit_ln(pr, banks, x_sb, nf, T, ones_bf, xt_bf, mode="ln",
                               name="moeln", x_bf=xt_bf, sq_bf=scr_bf, apply=False)

    def fxlo(e):
        e.waitall("te", "ve")
        e.tick(e.g.tensor_sub(scr_bf[:, 0:nf * T], x_sb[:, 0:nf * T], xt_bf[:, 0:nf * T]))
    pr.sec("vector", fxlo)

    if stop_after == "ln":
        def fstX(e):
            e.waitall("ve", "se", "te", "pe")
            for kj in range(nf):
                e.dma(xo_d[kj * 128:(kj + 1) * 128, :], x_sb[:, kj * T:(kj + 1) * T])
            e.need("dsp")
        pr.sec("sync", fstX)
        return pr
    ntb = cdiv(T, 512)
    l15 = pr.sbuf("l15", [15, T])
    lhl = pr.sbuf("lhl", [32, 2 * 512], BF16)
    csW = pr.sbuf("csW", [15, 1])
    rbc = pr.sbuf("rbc", [15, 1])
    onesf = pr.sbuf("onesf", [128, 1])
    mrow, rs = pr._moe_ln_stats

    def fv0(e):
        e.g.memset(lhl[0:32, :], 0.0)
        e.g.memset(onesf[:, :], 1.0)
        e.fence(e.g.tensor_copy(rWhi[:, :], rWf_sb[:, :]))
        e.tick(e.g.tensor_sub(rWlo[:, :], rWf_sb[:, :], rWhi[:, :]))
    pr.sec("vector", fv0)

    def fldrb(e):
        e.dma(rbc[0:15, 0:1], rb_d.rearrange("(e a) -> e a", a=1))
    pr.sec("gpsimd", fldrb)

    holder0 = {}

    def ftcs(e, holder0=holder0):
        e.waitall("dpe", "dsp", "ve")
        b, p = banks.get(e)
        ins = None
        for hl, wsb in ((0, rWhi), (1, rWlo)):
            for kj in range(nf):
                ins = e.g.matmul(p[0:15, 0:1], wsb[:, kj * 16:kj * 16 + 15],
                                 ones_bf[:, 0:1],
                                 start=(hl == 0 and kj == 0),
                                 stop=(hl == 1 and kj == nf - 1))
        e.tick(ins)
        holder0["b"] = (b, p)
    pr.sec("tensor", ftcs)

    def fvcs(e, holder0=holder0):
        e.waitall("te")
        b, p = holder0["b"]
        c = e.tick(e.g.tensor_copy(csW[0:15, 0:1], p[0:15, 0:1]))
        banks.release(b, "ve", c)
    pr.sec("vector", fvcs)

    for tb in range(ntb):
        t0, tl = tb * 512, min(512, T - tb * 512)
        holder = {}

        def ft(e, t0=t0, tl=tl, holder=holder):
            e.waitall("dpe", "dsp")
            b, p = banks.get(e)
            ins = None
            passes = [(rWhi, xt_bf), (rWhi, scr_bf), (rWlo, xt_bf)]
            for pi, (wsb, xb) in enumerate(passes):
                for kj in range(nf):
                    ins = e.g.matmul(p[0:15, 0:tl], wsb[:, kj * 16:kj * 16 + 15],
                                     xb[:, kj * T + t0:kj * T + t0 + tl],
                                     start=(pi == 0 and kj == 0),
                                     stop=(pi == 2 and kj == nf - 1))
            e.tick(ins)
            holder["b"] = (b, p)
        pr.sec("tensor", ft)

        def fv(e, t0=t0, tl=tl, holder=holder):
            e.waitall("te", "ve")
            b, p = holder["b"]
            lv = l15[0:15, t0:t0 + tl]
            e.fence(e.g.tensor_scalar_mul(lv, mrow[0:15, t0:t0 + tl], csW[0:15, 0:1]))
            e.fence(e.g.tensor_sub(lv, p[0:15, 0:tl], lv))
            e.fence(e.g.tensor_mul(lv, lv, rs[0:15, t0:t0 + tl]))
            c = e.tick(e.g.tensor_scalar_add(lv, lv, rbc[0:15, 0:1]))
            banks.release(b, "ve", c)
        pr.sec("vector", fv)

    ntt = T // 128
    bpt = pr.psum("bpt", [128, 512], BF16)
    for tt in range(ntt):
        def fhl(e, tt=tt):
            e.waitall("ve", "te")
            e.fence(e.g.tensor_copy(lhl[0:15, 0:128], l15[0:15, tt * 128:(tt + 1) * 128]))
            e.tick(e.g.tensor_sub(lhl[0:15, 512:512 + 128],
                                  l15[0:15, tt * 128:(tt + 1) * 128], lhl[0:15, 0:128]))
        pr.sec("vector", fhl)

        def ft(e, tt=tt):
            e.waitall("ve")
            e.g.transpose(bpt[0:128, 0:32],
                          lhl[0:32, 0:128], ident_bf[0:32, 0:32])
            e.tick(e.g.transpose(bpt[0:128, 32:64],
                                 lhl[0:32, 512:512 + 128],
                                 ident_bf[0:32, 0:32]))
        pr.sec("tensor", ft)

        def fv1(e, tt=tt):
            e.waitall("te", "se")
            e.fence(e.g.tensor_copy(lg[:, 0:15], bpt[0:128, 0:15]))
            e.fence(e.g.tensor_add(lg[:, 0:15], lg[:, 0:15], bpt[0:128, 32:47]))
            if tt == 1 and hasattr(pr, "_dbg_lgraw"):
                e.g.tensor_copy(pr._dbg_lgraw[:, 0:16], bpt[0:128, 0:16])
                e.g.tensor_copy(pr._dbg_lgraw[:, 16:32], bpt[0:128, 32:48])
            e.fence(e.g.reduce_max(mx[:, 0:1], lg[:, 0:15], axis=AX, negate=True))
        pr.sec("vector", fv1)

        def fs1(e):
            e.waitall("ve")
            e.tick(e.g.activation(lg[:, 0:E_], lg[:, 0:E_], AF.Exp, bias=mx[:, 0:1]))
        pr.sec("scalar", fs1)

        def fv2(e):
            e.waitall("se", "te")
            e.fence(e.g.reduce_sum(mx[:, 4:5], lg[:, 0:E_], axis=AX))
            e.fence(e.g.reciprocal(mx[:, 4:5], mx[:, 4:5]))
            e.fence(e.g.tensor_scalar_mul(lg[:, 0:E_], lg[:, 0:E_], mx[:, 4:5]))
            e.fence(e.g.memset(cw_tile[:, :], 0.0))
            dbgp = getattr(pr, "_dbg_lgraw", None)
            if dbgp is not None:
                e.g.tensor_copy(dbgp[:, 0:15], lg[:, 0:15])
            for _k in range(K_):
                e.fence(e.g.reduce_max(mx[:, 8:9], lg[:, 0:E_], axis=AX))
                e.fence(e.g.tensor_single_scalar(lg[:, 16:16 + E_], lg[:, 0:E_],
                                                 mx[:, 8:9], ALU.is_ge))
                e.fence(e.g.scalar_tensor_tensor(cw_tile[:, 0:E_], lg[:, 16:16 + E_],
                                                 mx[:, 8:9], cw_tile[:, 0:E_],
                                                 ALU.mult, ALU.add))
                e.fence(e.g.scalar_tensor_tensor(lg[:, 0:E_], lg[:, 16:16 + E_], -2.0,
                                                 lg[:, 0:E_], ALU.mult, ALU.add))
                if dbgp is not None and _k == 0:
                    e.g.tensor_copy(dbgp[:, 16:17], mx[:, 8:9])
                    e.g.tensor_copy(dbgp[:, 17:32], lg[:, 16:16 + E_])
        pr.sec("vector", fv2)

        def fvc(e):
            e.waitall("ve")
            e.tick(e.g.tensor_copy(cwt_b[:, :], cw_tile[:, 0:16]))
        pr.sec("vector", fvc)

        def ft2(e):
            e.waitall("ve")
            e.tick(e.g.transpose(bpt[0:16, 32:160], cwt_b[:, :], ident_bf[:, :]))
        pr.sec("tensor", ft2)

        def fv3(e, tt=tt):
            e.waitall("te")
            e.tick(e.g.tensor_copy(cwT_bf[0:16, tt * 128:(tt + 1) * 128], bpt[0:16, 32:160]))
        pr.sec("vector", fv3)

    if stop_after == "router":
        def fstX(e):
            e.waitall("ve", "se", "te", "pe")
            for kj in range(nf):
                e.dma(xo_d[kj * 128:(kj + 1) * 128, :], x_sb[:, kj * T:(kj + 1) * T])
            e.need("dsp")
        pr.sec("sync", fstX)
        return pr
    emit_ln_apply(pr, x_sb, nf, T, xt_bf, pr._moe_ln_stats[0], pr._moe_ln_stats[1],
                  mode="ln")

    ntb = cdiv(T, 512)
    ksall = [(k, k, 128) for k in range(nf)]

    def hidden_pass(wsb, bias_c, tag):
        for mj in range(nh):
            for tb in range(ntb):
                t0, tl = tb * 512, min(512, T - tb * 512)
                holder = {}

                def ft(e, mj=mj, t0=t0, tl=tl, holder=holder):
                    e.waitall("dpe", "pe")
                    b, p = banks.get(e)
                    e.tick(mm(e, p, wsb, 1024, mj * 128, 128, xt_bf, T, t0, tl, ksall))
                    holder["b"] = (b, p)
                pr.sec("tensor", ft)

                def fs(e, mj=mj, t0=t0, tl=tl, holder=holder, bias_c=bias_c):
                    e.waitall("te")
                    b, p = holder["b"]
                    c = e.tick(e.g.activation(scr_bf[:, mj * T + t0:mj * T + t0 + tl],
                                              p[:, 0:tl], AF.Relu, bias=bias_c(mj)))
                    banks.release(b, "se", c)
                pr.sec("scalar", fs)

    def down_pass(wsb, first):
        for mj in range(nf):
            for tb in range(ntb):
                t0, tl = tb * 512, min(512, T - tb * 512)
                holder = {}

                def ft(e, mj=mj, t0=t0, tl=tl, holder=holder, first=first):
                    e.waitall("se" if first else "pe")
                    b, p = banks.get(e)
                    ins = mm(e, p, wsb, 1024, mj * 128, 128, scr_bf, T, t0, tl,
                             [(k, k, 128) for k in range(nh)], stop=not first)
                    if first:
                        ins = e.g.matmul(p[:, 0:tl], eb2_sb[0:E_, mj * 128:(mj + 1) * 128],
                                         cwT_bf[0:E_, t0:t0 + tl], start=False, stop=True)
                    e.tick(ins)
                    holder["b"] = (b, p)
                pr.sec("tensor", ft)

                def fv(e, mj=mj, t0=t0, tl=tl, holder=holder, first=first):
                    e.waitall("te")
                    b, p = holder["b"]
                    xv = x_sb[:, mj * T + t0:mj * T + t0 + tl]
                    if first:
                        ins = e.g.scalar_tensor_tensor(xv, p[:, 0:tl], sb2_c[:, mj:mj + 1],
                                                       xv, ALU.add, ALU.add)
                    else:
                        ins = e.g.tensor_add(xv, xv, p[:, 0:tl])
                    c = e.tick(ins)
                    banks.release(b, "ve", c)
                pr.sec("vector", fv)

    hidden_pass(wbuf[0], lambda mj: sb1_c[:, mj:mj + 1], "sh")
    down_pass(wbuf[1], True)

    if stop_after == "shared":
        def fstX(e):
            e.waitall("ve", "se", "te", "pe")
            for kj in range(nf):
                e.dma(xo_d[kj * 128:(kj + 1) * 128, :], x_sb[:, kj * T:(kj + 1) * T])
            e.need("dsp")
        pr.sec("sync", fstX)
        return pr
    for ei in range(E_):
        i1, i2 = 0, 1
        b1w, b2w = wbuf[i1], wbuf[i2]

        def fld(e, ei=ei, i1=i1, i2=i2, b1w=b1w, b2w=b2w):
            if wdep[i1] is not None:
                e.need(*wdep[i1])
            else:
                e.waitall("te")
            for kj in range(nf):
                e.dma(b1w[:, kj * 1024:(kj + 1) * 1024], eW1_d[ei, kj * 128:(kj + 1) * 128, :])
            if wdep[i2] is not None:
                e.need(*wdep[i2])
            else:
                e.waitall("te")
            for kj in range(nh):
                e.dma(b2w[:, kj * 1024:(kj + 1) * 1024], eW2_d[ei, kj * 128:(kj + 1) * 128, :])
        pr.sec("gpsimd", fld)
        w1dep = pr.cnt["dpe"] - 16 * nh
        w2dep = pr.cnt["dpe"]

        holder = {}

        def ftb(e, ei=ei, holder=holder):
            e.waitall("ve")
            bs = []
            for tb in range(ntb):
                t0, tl = tb * 512, min(512, T - tb * 512)
                b, p = banks.get(e)
                e.tick(e.g.matmul(p[:, 0:tl], bc15_sb[0:E_, ei * 128:(ei + 1) * 128],
                                  cwT_bf[0:E_, t0:t0 + tl], start=True, stop=True))
                bs.append((b, p, t0, tl))
            holder["bs"] = bs
        pr.sec("tensor", ftb)

        def fvb(e, holder=holder):
            e.waitall("te", "pe")
            for (b, p, t0, tl) in holder["bs"]:
                c = e.tick(e.g.tensor_copy(cwb_bf[:, t0:t0 + tl], p[:, 0:tl]))
                banks.release(b, "ve", c)
        pr.sec("vector", fvb)

        for mj in range(nh):
            for tb in range(ntb):
                t0, tl = tb * 512, min(512, T - tb * 512)
                holder2 = {}

                def ft(e, mj=mj, t0=t0, tl=tl, holder2=holder2, w1dep=w1dep):
                    e.need("dpe", w1dep)
                    e.waitall("pe")
                    b, p = banks.get(e)
                    e.tick(mm(e, p, b1w, 1024, mj * 128, 128, xt_bf, T, t0, tl, ksall))
                    holder2["b"] = (b, p)
                pr.sec("tensor", ft)

                def fs(e, mj=mj, t0=t0, tl=tl, holder2=holder2, ei=ei):
                    e.waitall("te")
                    b, p = holder2["b"]
                    c = e.tick(e.g.activation(scr_bf[:, mj * T + t0:mj * T + t0 + tl],
                                              p[:, 0:tl], AF.Relu,
                                              bias=eb1_c[:, ei * nh + mj:ei * nh + mj + 1]))
                    banks.release(b, "se", c)
                pr.sec("scalar", fs)

        def fpool2(e):
            e.waitall("se", "ve")
            last = None
            for mj in range(nh):
                last = e.g.tensor_mul(scr_bf[:, mj * T:(mj + 1) * T],
                                      scr_bf[:, mj * T:(mj + 1) * T], cwb_bf[:, 0:T])
            e.tick(last)
        pr.sec("gpsimd", fpool2)

        for mj in range(nf):
            for tb in range(ntb):
                t0, tl = tb * 512, min(512, T - tb * 512)
                holder2 = {}

                def ft(e, mj=mj, t0=t0, tl=tl, holder2=holder2, w2dep=w2dep, b2w=b2w):
                    e.need("dpe", w2dep)
                    e.waitall("pe")
                    b, p = banks.get(e)
                    e.tick(mm(e, p, b2w, 1024, mj * 128, 128, scr_bf, T, t0, tl,
                              [(k, k, 128) for k in range(nh)]))
                    holder2["b"] = (b, p)
                pr.sec("tensor", ft)

                def fv(e, mj=mj, t0=t0, tl=tl, holder2=holder2):
                    e.waitall("te")
                    b, p = holder2["b"]
                    xv = x_sb[:, mj * T + t0:mj * T + t0 + tl]
                    c = e.tick(e.g.tensor_add(xv, xv, p[:, 0:tl]))
                    banks.release(b, "ve", c)
                pr.sec("vector", fv)

        wdep[i1] = ("te", pr.cnt["te"])
        wdep[i2] = ("te", pr.cnt["te"])

    def fst(e):
        e.waitall("ve")
        for kj in range(nf):
            e.dma(xo_d[kj * 128:(kj + 1) * 128, :], x_sb[:, kj * T:(kj + 1) * T])
        e.need("dsp")
    pr.sec("sync", fst)
    if dbg:
        dxt = pr.dout("dbg_xt", [D_MODEL, T], BF16)
        dcw = pr.dout("dbg_cw", [16, T], BF16)
        dl15 = pr.dout("dbg_l15", [15, T])
        dlg = pr.dout("dbg_lg", [128, 16])
        dcwt = pr.dout("dbg_cwtile", [128, 16])

        def fdbg(e):
            e.waitall("ve", "se", "pe", "te")
            for kj in range(nf):
                e.dma(dxt[kj * 128:(kj + 1) * 128, :], xt_bf[:, kj * T:(kj + 1) * T])
            e.dma(dcw, cwT_bf[0:16, :])
            e.dma(dl15, l15[0:15, :])
            e.dma(dlg, lg[:, 0:16])
            e.dma(dcwt, cw_tile[:, 0:16])
            e.dma(pr.dout("dbg_lgraw", [128, 32]), pr._dbg_lgraw[:, :])
            e.need("dsp")
        pr.sec("sync", fdbg)
    return pr



def build_moe32(T=2048):
    pr = Prog("moe32")
    nf, E_, K_ = 8, 32, 4
    x_d = pr.din("x", [D_MODEL, T])
    ones_d = pr.din("ones_bf", [128, 128], BF16)
    identbf_d = pr.din("ident_bf", [128, 128], BF16)
    bc32_d = pr.din("bcast32", [32, 32 * 128], BF16)
    rW_d = pr.din("rW", [D_MODEL, E_])
    rb_d = pr.din("rb", [E_])
    gate_d = pr.din("gate", [E_, D_MODEL, 128])
    up_d = pr.din("up", [E_, D_MODEL, 128])
    down_d = pr.din("down", [E_, 128, D_MODEL])
    sg_d = pr.din("sgate", [D_MODEL, 512])
    su_d = pr.din("sup", [D_MODEL, 512])
    sd_d = pr.din("sdown", [512, D_MODEL])
    xo_d = pr.dout("xo", [D_MODEL, T])

    banks = Banks(pr, 6)
    x_sb = pr.sbuf("x", [128, nf * T])
    xt_bf = pr.sbuf("xt", [128, nf * T], BF16)
    scr_bf = pr.sbuf("scr", [128, nf * T], BF16)
    wg = pr.sbuf("wg", [128, 8 * 128], BF16)
    wu = pr.sbuf("wu", [128, 8 * 128], BF16)
    wd = pr.sbuf("wd", [128, 1 * 1024], BF16)
    wsh = pr.sbuf("wsh", [128, 8 * 512], BF16)
    wsh2 = pr.sbuf("wsh2", [128, 8 * 512], BF16)
    wsd = pr.sbuf("wsd", [128, 4 * 1024], BF16)
    ones_bf = pr.sbuf("ones", [128, 128], BF16)
    ident_bf = pr.sbuf("identbf", [128, 128], BF16)
    rWf_sb = pr.sbuf("rWf", [128, nf * 32])
    rWhi = pr.sbuf("rWhi", [128, nf * 32], BF16)
    rWlo = pr.sbuf("rWlo", [128, nf * 32], BF16)
    l32 = pr.sbuf("l32", [32, T])
    lhl = pr.sbuf("lhl", [32, 2 * 512], BF16)
    csW = pr.sbuf("csW", [32, 1])
    rbc = pr.sbuf("rbc", [32, 1])
    cwT_bf = pr.sbuf("cwTb", [32, T], BF16)
    cwb_bf = pr.sbuf("cwb", [128, T], BF16)
    lg = pr.sbuf("lg", [128, 36 * 2])
    mx = pr.sbuf("mx", [128, 12])
    cw_tile = pr.sbuf("cwt", [128, 32])
    cwt_b = pr.sbuf("cwtb16", [128, 32], BF16)
    bc32_sb = pr.sbuf("bc32", [32, 32 * 128], BF16)
    hsg = scr_bf[:, T:5 * T]

    def fldsp(e):
        for kj in range(nf):
            e.dma(x_sb[:, kj * T:(kj + 1) * T], x_d[kj * 128:(kj + 1) * 128, :])
        e.dma(ones_bf[:, :], ones_d)
        e.dma(ident_bf[:, :], identbf_d)
        e.dma(bc32_sb[0:32, :], bc32_d)
    pr.sec("sync", fldsp)

    def fldpe(e):
        for kj in range(nf):
            e.dma(rWf_sb[:, kj * 32:(kj + 1) * 32], rW_d[kj * 128:(kj + 1) * 128, :])
        e.dma(rbc[0:32, 0:1], rb_d.rearrange("(e a) -> e a", a=1))
        for kj in range(nf):
            e.dma(wsh[:, kj * 512:(kj + 1) * 512], sg_d[kj * 128:(kj + 1) * 128, :])
            e.dma(wsh2[:, kj * 512:(kj + 1) * 512], su_d[kj * 128:(kj + 1) * 128, :])
        for kj in range(4):
            e.dma(wsd[:, kj * 1024:(kj + 1) * 1024], sd_d[kj * 128:(kj + 1) * 128, :])
    pr.sec("gpsimd", fldpe)

    def fcast(e):
        e.tick(e.g.tensor_copy(xt_bf[:, :], x_sb[:, 0:nf * T]))
    pr.sec("gpsimd", fcast)
    mrow, rs = emit_ln(pr, banks, x_sb, nf, T, ones_bf, xt_bf, mode="ln",
                       name="m32ln", x_bf=xt_bf, sq_bf=scr_bf, apply=False)

    def fxlo(e):
        e.fence(e.g.tensor_sub(scr_bf[:, 0:nf * T], x_sb[:, 0:nf * T], xt_bf[:, 0:nf * T]))
    pr.sec("vector", fxlo)

    def fv0(e):
        e.g.memset(lhl[0:32, :], 0.0)
        e.fence(e.g.tensor_copy(rWhi[:, :], rWf_sb[:, :]))
        e.tick(e.g.tensor_sub(rWlo[:, :], rWf_sb[:, :], rWhi[:, :]))
    pr.sec("vector", fv0)

    holder0 = {}

    def ftcs(e, holder0=holder0):
        b, p = banks.get(e)
        ins = None
        for hl, wsb in ((0, rWhi), (1, rWlo)):
            for kj in range(nf):
                ins = e.g.matmul(p[0:32, 0:1], wsb[:, kj * 32:kj * 32 + 32],
                                 ones_bf[:, 0:1],
                                 start=(hl == 0 and kj == 0),
                                 stop=(hl == 1 and kj == nf - 1))
        e.tick(ins)
        holder0["b"] = (b, p)
    pr.sec("tensor", ftcs)

    def fvcs(e, holder0=holder0):
        b, p = holder0["b"]
        c = e.tick(e.g.tensor_copy(csW[0:32, 0:1], p[0:32, 0:1]))
        banks.release(b, "ve", c)
    pr.sec("vector", fvcs)

    ntb = cdiv(T, 512)
    for tb in range(ntb):
        t0, tl = tb * 512, min(512, T - tb * 512)
        holder = {}

        def ft(e, t0=t0, tl=tl, holder=holder):
            b, p = banks.get(e)
            ins = None
            passes = [(rWhi, xt_bf), (rWhi, scr_bf), (rWlo, xt_bf)]
            for pi, (wsb, xb) in enumerate(passes):
                for kj in range(nf):
                    ins = e.g.matmul(p[0:32, 0:tl], wsb[:, kj * 32:(kj + 1) * 32],
                                     xb[:, kj * T + t0:kj * T + t0 + tl],
                                     start=(pi == 0 and kj == 0),
                                     stop=(pi == 2 and kj == nf - 1))
            e.tick(ins)
            holder["b"] = (b, p)
        pr.sec("tensor", ft)

        def fv(e, t0=t0, tl=tl, holder=holder):
            b, p = holder["b"]
            lv = l32[0:32, t0:t0 + tl]
            e.fence(e.g.tensor_scalar_mul(lv, mrow[0:32, t0:t0 + tl], csW[0:32, 0:1]))
            e.fence(e.g.tensor_sub(lv, p[0:32, 0:tl], lv))
            e.fence(e.g.tensor_mul(lv, lv, rs[0:32, t0:t0 + tl]))
            c = e.tick(e.g.tensor_scalar_add(lv, lv, rbc[0:32, 0:1]))
            banks.release(b, "ve", c)
        pr.sec("vector", fv)

    ntt = T // 128
    bpt = pr.psum("bpt", [128, 512], BF16)
    for tt in range(ntt):
        def fhl(e, tt=tt):
            e.fence(e.g.tensor_copy(lhl[0:32, 0:128], l32[0:32, tt * 128:(tt + 1) * 128]))
            e.tick(e.g.tensor_sub(lhl[0:32, 512:512 + 128],
                                  l32[0:32, tt * 128:(tt + 1) * 128], lhl[0:32, 0:128]))
        pr.sec("vector", fhl)

        def ft(e, tt=tt):
            e.g.transpose(bpt[0:128, 0:32], lhl[0:32, 0:128], ident_bf[0:32, 0:32])
            e.tick(e.g.transpose(bpt[0:128, 32:64], lhl[0:32, 512:512 + 128],
                                 ident_bf[0:32, 0:32]))
        pr.sec("tensor", ft)

        def fv1(e):
            e.fence(e.g.tensor_copy(lg[:, 0:32], bpt[0:128, 0:32]))
            e.fence(e.g.tensor_add(lg[:, 0:32], lg[:, 0:32], bpt[0:128, 32:64]))
            e.fence(e.g.reduce_max(mx[:, 0:1], lg[:, 0:32], axis=AX, negate=True))
        pr.sec("vector", fv1)

        def fs1(e):
            e.tick(e.g.activation(lg[:, 0:E_], lg[:, 0:E_], AF.Exp, bias=mx[:, 0:1]))
        pr.sec("scalar", fs1)

        def fv2(e):
            e.fence(e.g.reduce_sum(mx[:, 4:5], lg[:, 0:E_], axis=AX))
            e.fence(e.g.reciprocal(mx[:, 4:5], mx[:, 4:5]))
            e.fence(e.g.tensor_scalar_mul(lg[:, 0:E_], lg[:, 0:E_], mx[:, 4:5]))
            e.fence(e.g.memset(cw_tile[:, :], 0.0))
            for _k in range(K_):
                e.fence(e.g.reduce_max(mx[:, 8:9], lg[:, 0:E_], axis=AX))
                e.fence(e.g.tensor_single_scalar(lg[:, 36:36 + E_], lg[:, 0:E_],
                                                 mx[:, 8:9], ALU.is_ge))
                e.fence(e.g.scalar_tensor_tensor(cw_tile[:, 0:E_], lg[:, 36:36 + E_],
                                                 mx[:, 8:9], cw_tile[:, 0:E_],
                                                 ALU.mult, ALU.add))
                e.fence(e.g.scalar_tensor_tensor(lg[:, 0:E_], lg[:, 36:36 + E_], -2.0,
                                                 lg[:, 0:E_], ALU.mult, ALU.add))
            e.fence(e.g.reduce_sum(mx[:, 4:5], cw_tile[:, 0:E_], axis=AX))
            e.fence(e.g.tensor_single_scalar(mx[:, 4:5], mx[:, 4:5], 1e-12, ALU.max))
            e.fence(e.g.reciprocal(mx[:, 4:5], mx[:, 4:5]))
            e.fence(e.g.tensor_scalar_mul(cw_tile[:, 0:E_], cw_tile[:, 0:E_], mx[:, 4:5]))
            e.fence(e.g.tensor_copy(cwt_b[:, :], cw_tile[:, 0:32]))
        pr.sec("vector", fv2)

        def ft2(e):
            e.tick(e.g.transpose(bpt[0:32, 64:192], cwt_b[:, :], ident_bf[:, :]))
        pr.sec("tensor", ft2)

        def fv3(e, tt=tt):
            e.tick(e.g.tensor_copy(cwT_bf[0:32, tt * 128:(tt + 1) * 128], bpt[0:32, 64:192]))
        pr.sec("vector", fv3)

    emit_ln_apply(pr, x_sb, nf, T, xt_bf, mrow, rs, mode="ln")

    for mj in range(4):
        for tb in range(ntb):
            t0, tl = tb * 512, min(512, T - tb * 512)
            holder = {}

            def ft(e, mj=mj, t0=t0, tl=tl, holder=holder):
                b, p = banks.get(e)
                e.tick(mm(e, p, wsh, 512, mj * 128, 128, xt_bf, T, t0, tl,
                          [(k, k, 128) for k in range(nf)]))
                b2, p2 = banks.get(e)
                e.tick(mm(e, p2, wsh2, 512, mj * 128, 128, xt_bf, T, t0, tl,
                          [(k, k, 128) for k in range(nf)]))
                holder["b"] = (b, p, b2, p2)
            pr.sec("tensor", ft)

            def fs(e, mj=mj, t0=t0, tl=tl, holder=holder):
                b, p, b2, p2 = holder["b"]
                c = e.tick(e.g.activation(hsg[:, mj * T + t0:mj * T + t0 + tl],
                                          p[:, 0:tl], AF.Silu))
                banks.release(b, "se", c)
            pr.sec("scalar", fs)

            def fv(e, mj=mj, t0=t0, tl=tl, holder=holder):
                b, p, b2, p2 = holder["b"]
                hv = hsg[:, mj * T + t0:mj * T + t0 + tl]
                import os
                if os.environ.get("DBGUP"):
                    c = e.tick(e.g.tensor_copy(hv, p2[:, 0:tl]))
                else:
                    c = e.tick(e.g.tensor_mul(hv, hv, p2[:, 0:tl]))
                banks.release(b2, "ve", c)
            pr.sec("vector", fv)
    for mj in range(nf):
        for tb in range(ntb):
            t0, tl = tb * 512, min(512, T - tb * 512)
            holder = {}

            def ft(e, mj=mj, t0=t0, tl=tl, holder=holder):
                b, p = banks.get(e)
                e.tick(mm(e, p, wsd, 1024, mj * 128, 128, hsg, T, t0, tl,
                          [(k, k, 128) for k in range(4)]))
                holder["b"] = (b, p)
            pr.sec("tensor", ft)

            def fv(e, mj=mj, t0=t0, tl=tl, holder=holder):
                b, p = holder["b"]
                xv = x_sb[:, mj * T + t0:mj * T + t0 + tl]
                c = e.tick(e.g.tensor_add(xv, xv, p[:, 0:tl]))
                banks.release(b, "ve", c)
            pr.sec("vector", fv)

    wde = [(wg, wu, wd)]
    wdep = [None]
    for ei in range(E_):
        par = 0
        g_w, u_w, d_w = wde[par]

        def fld(e, ei=ei, par=par, g_w=g_w, u_w=u_w, d_w=d_w):
            if wdep[par] is not None:
                e.need(*wdep[par])
            else:
                e.waitall("te")
            for kj in range(nf):
                e.dma(g_w[:, kj * 128:(kj + 1) * 128], gate_d[ei, kj * 128:(kj + 1) * 128, :])
                e.dma(u_w[:, kj * 128:(kj + 1) * 128], up_d[ei, kj * 128:(kj + 1) * 128, :])
            e.dma(d_w[:, 0:1024], down_d[ei, 0:128, :])
        pr.sec("gpsimd", fld, auto=False)
        wldep = pr.cnt["dpe"]

        holder = {}

        def ftb(e, ei=ei, holder=holder):
            bs = []
            for tb in range(ntb):
                t0, tl = tb * 512, min(512, T - tb * 512)
                b, p = banks.get(e)
                e.tick(e.g.matmul(p[:, 0:tl], bc32_sb[0:E_, ei * 128:(ei + 1) * 128],
                                  cwT_bf[0:E_, t0:t0 + tl], start=True, stop=True))
                bs.append((b, p, t0, tl))
            holder["bs"] = bs
        pr.sec("tensor", ftb)

        def fvb(e, holder=holder):
            for (b, p, t0, tl) in holder["bs"]:
                c = e.tick(e.g.tensor_copy(cwb_bf[:, t0:t0 + tl], p[:, 0:tl]))
                banks.release(b, "ve", c)
        pr.sec("vector", fvb)

        for tb in range(ntb):
            t0, tl = tb * 512, min(512, T - tb * 512)
            holder2 = {}

            def ft(e, t0=t0, tl=tl, holder2=holder2, wldep=wldep, g_w=g_w, u_w=u_w):
                e.need("dpe", wldep)
                b, p = banks.get(e)
                e.tick(mm(e, p, g_w, 128, 0, 128, xt_bf, T, t0, tl,
                          [(k, k, 128) for k in range(nf)]))
                b2, p2 = banks.get(e)
                e.tick(mm(e, p2, u_w, 128, 0, 128, xt_bf, T, t0, tl,
                          [(k, k, 128) for k in range(nf)]))
                holder2["b"] = (b, p, b2, p2)
            pr.sec("tensor", ft)

            def fs(e, t0=t0, tl=tl, holder2=holder2):
                b, p, b2, p2 = holder2["b"]
                c = e.tick(e.g.activation(scr_bf[:, t0:t0 + tl], p[:, 0:tl], AF.Silu))
                banks.release(b, "se", c)
            pr.sec("scalar", fs)

            def fv(e, t0=t0, tl=tl, holder2=holder2):
                b, p, b2, p2 = holder2["b"]
                hv = scr_bf[:, t0:t0 + tl]
                e.fence(e.g.tensor_mul(hv, hv, p2[:, 0:tl]))
                c = e.tick(e.g.tensor_mul(hv, hv, cwb_bf[:, t0:t0 + tl]))
                banks.release(b2, "ve", c)
            pr.sec("vector", fv)

        for mj in range(nf):
            for tb in range(ntb):
                t0, tl = tb * 512, min(512, T - tb * 512)
                holder2 = {}

                def ft(e, mj=mj, t0=t0, tl=tl, holder2=holder2, d_w=d_w):
                    b, p = banks.get(e)
                    e.tick(mm(e, p, d_w, 1024, mj * 128, 128, scr_bf, T, t0, tl,
                              [(0, 0, 128)]))
                    holder2["b"] = (b, p)
                pr.sec("tensor", ft)

                def fv(e, mj=mj, t0=t0, tl=tl, holder2=holder2):
                    b, p = holder2["b"]
                    xv = x_sb[:, mj * T + t0:mj * T + t0 + tl]
                    c = e.tick(e.g.tensor_add(xv, xv, p[:, 0:tl]))
                    banks.release(b, "ve", c)
                pr.sec("vector", fv)
        wdep[par] = ("te", pr.cnt["te"])

    def fst(e):
        e.waitall("ve")
        for kj in range(nf):
            e.dma(xo_d[kj * 128:(kj + 1) * 128, :], x_sb[:, kj * T:(kj + 1) * T])
        e.need("dsp")
    pr.sec("sync", fst)
    return pr



def build_l0post1(T=2048):
    pr = Prog("l0post1")
    y2_d = pr.din("y2", [2048, T], BF16)
    ones_d = pr.din("ones_bf", [128, 128], BF16)
    ssmw_d = pr.din("ssm_w", [2048])
    wo_d = pr.din("w_out", [2048, D_MODEL])
    mix_d = pr.dout("mix", [D_MODEL, T])

    banks = Banks(pr, 6)
    y2_sb = pr.sbuf("y2", [128, 16 * T], BF16)
    yb = pr.sbuf("yb", [128, 16 * T], BF16)
    wbig = pr.sbuf("wbig", [128, 16 * 1024], BF16)
    ones_bf = pr.sbuf("ones", [128, 128], BF16)
    ssm_c = pr.sbuf("ssmc", [128, 16])
    rsy = pr.sbuf("rsy", [128, T])
    stage = pr.sbuf("stage", [128, 512])

    def fldsp(e):
        for kj in range(16):
            e.dma(y2_sb[:, kj * T:(kj + 1) * T], y2_d[kj * 128:(kj + 1) * 128, :])
        e.dma(ones_bf[:, :], ones_d)
    pr.sec("sync", fldsp)

    def fldpe(e):
        for kj in range(16):
            e.dma(wbig[:, kj * 1024:(kj + 1) * 1024], wo_d[kj * 128:(kj + 1) * 128, :])
        for kj in range(16):
            e.dma(ssm_c[:, kj:kj + 1], ssmw_d.rearrange("(k p) -> k p", p=128)[kj].rearrange("(p a) -> p a", a=1))
    pr.sec("gpsimd", fldpe)

    def fsq(e):
        e.tick(e.g.tensor_mul(yb[:, 0:16 * T], y2_sb[:, 0:16 * T], y2_sb[:, 0:16 * T]))
    pr.sec("gpsimd", fsq)
    ntb = cdiv(T, 512)
    for tb in range(ntb):
        t0, tl = tb * 512, min(512, T - tb * 512)
        holder = {}

        def ftens(e, t0=t0, tl=tl, holder=holder):
            b2, p2 = banks.get(e)
            ins = None
            for kj in range(16):
                ins = e.g.matmul(p2[:, 0:tl], ones_bf[:, :],
                                 yb[:, kj * T + t0:kj * T + t0 + tl],
                                 start=(kj == 0), stop=(kj == 15))
            e.tick(ins)
            holder["b"] = (b2, p2)
        pr.sec("tensor", ftens)

        def fvec(e, t0=t0, tl=tl, holder=holder):
            b2, p2 = holder["b"]
            c = e.tick(e.g.tensor_scalar_mul(rsy[:, t0:t0 + tl], p2[:, 0:tl], 1.0 / 2048))
            banks.release(b2, "ve", c)
        pr.sec("vector", fvec)
    cc = pr.ccol(1e-6)

    def fact(e):
        e.tick(e.g.activation(rsy[:, 0:T], rsy[:, 0:T], AF.Sqrt, bias=cc))
    pr.sec("scalar", fact)

    def frecip(e):
        e.fence(e.g.reciprocal(rsy[:, 0:T], rsy[:, 0:T]))
    pr.sec("vector", frecip)

    def fnorm(e):
        last = None
        for kj in range(16):
            last = e.g.scalar_tensor_tensor(yb[:, kj * T:(kj + 1) * T],
                                            y2_sb[:, kj * T:(kj + 1) * T],
                                            ssm_c[:, kj:kj + 1], rsy[:, 0:T],
                                            ALU.mult, ALU.mult)
        e.tick(last)
    pr.sec("vector", fnorm)

    for mj in range(8):
        for tb in range(ntb):
            t0, tl = tb * 512, min(512, T - tb * 512)
            holder = {}

            def ft(e, mj=mj, t0=t0, tl=tl, holder=holder):
                b, p = banks.get(e)
                e.tick(mm(e, p, wbig, 1024, mj * 128, 128, yb, T, t0, tl,
                          [(k, k, 128) for k in range(16)]))
                holder["b"] = (b, p)
            pr.sec("tensor", ft)

            def fv(e, mj=mj, t0=t0, tl=tl, holder=holder):
                b, p = holder["b"]
                c = e.tick(e.g.tensor_copy(stage[:, 0:tl], p[:, 0:tl]))
                banks.release(b, "ve", c)
            pr.sec("vector", fv)

            def fso(e, mj=mj, t0=t0, tl=tl):
                e.dma(mix_d[mj * 128:(mj + 1) * 128, t0:t0 + tl], stage[:, 0:tl])
            pr.sec("sync", fso)

    def fend(e):
        e.need("dsp")
    pr.sec("sync", fend)
    return pr



def build_l0post2(T=2048):
    pr = Prog("l0post2")
    nf = 8
    xo_outer_d = pr.din("x_outer", [D_MODEL, T])
    hid_d = pr.din("hid", [D_MODEL, T])
    mix_d = pr.din("mix", [D_MODEL, T])
    ones_d = pr.din("ones_bf", [128, 128], BF16)
    postw_d = pr.din("post_w", [D_MODEL])
    wg_d = pr.din("w_gate", [D_MODEL, 2752])
    wu_d = pr.din("w_up", [D_MODEL, 2752])
    wd_d = pr.din("w_down", [2752, D_MODEL])
    xo_d = pr.dout("xo", [D_MODEL, T])

    banks = Banks(pr, 6)
    x_sb = pr.sbuf("x", [128, nf * T])
    hh_bf = pr.sbuf("hh", [128, nf * T], BF16)
    sqs = pr.sbuf("sqs", [128, nf * T], BF16)
    hidh = pr.sbuf("hidh", [128, 11 * T], BF16)
    wstr = [pr.sbuf("wstr", [128, 11 * 128], BF16) for _ in range(4)]
    wdep = [None] * 4
    ones_bf = pr.sbuf("ones", [128, 128], BF16)
    post_c = pr.sbuf("postc", [128, nf])
    stage = pr.sbuf("stage", [128, 2048])

    def fld2(e):
        e.dma(ones_bf[:, :], ones_d)
        for kj in range(nf):
            e.dma(x_sb[:, kj * T:(kj + 1) * T], hid_d[kj * 128:(kj + 1) * 128, :])
    pr.sec("sync", fld2)

    for kj in range(nf):
        def fldm(e, kj=kj):
            e.dma(stage[:, 0:T], mix_d[kj * 128:(kj + 1) * 128, :])
        pr.sec("sync", fldm)

        def fadd(e, kj=kj):
            xv = x_sb[:, kj * T:(kj + 1) * T]
            e.tick(e.g.tensor_add(xv, xv, stage[:, 0:T]))
        pr.sec("vector", fadd)

    def fldpe(e):
        for kj in range(nf):
            e.dma(post_c[:, kj:kj + 1], postw_d.rearrange("(k p) -> k p", p=128)[kj].rearrange("(p a) -> p a", a=1))
    pr.sec("gpsimd", fldpe)

    def fcast2(e):
        e.tick(e.g.tensor_copy(hh_bf[:, :], x_sb[:, 0:nf * T]))
    pr.sec("gpsimd", fcast2)
    emit_ln(pr, banks, x_sb, nf, T, ones_bf, hh_bf, mode="rms", gain_cols=post_c,
            eps=1e-6, name="postrms", x_bf=hh_bf, sq_bf=sqs)

    ntb = cdiv(T, 512)
    mlist = [(mj, min(128, 2752 - mj * 128)) for mj in range(22)]
    for half in range(2):
        hml = mlist[half * 11:(half + 1) * 11]
        for li, (mj, mlen) in enumerate(hml):
            bg = wstr[(2 * li) % 4]
            bu = wstr[(2 * li + 1) % 4]
            ig, iu = (2 * li) % 4, (2 * li + 1) % 4

            def fldw(e, mj=mj, mlen=mlen, bg=bg, bu=bu, ig=ig, iu=iu):
                if wdep[ig] is not None:
                    e.need(*wdep[ig])
                else:
                    e.waitall("te")
                for kj in range(nf):
                    e.dma(bg[:, kj * 128:kj * 128 + mlen],
                          wg_d[kj * 128:(kj + 1) * 128, mj * 128:mj * 128 + mlen])
                if wdep[iu] is not None:
                    e.need(*wdep[iu])
                for kj in range(nf):
                    e.dma(bu[:, kj * 128:kj * 128 + mlen],
                          wu_d[kj * 128:(kj + 1) * 128, mj * 128:mj * 128 + mlen])
            pr.sec("gpsimd", fldw, auto=False)
            wl = pr.cnt["dpe"]

            for tb in range(ntb):
                t0, tl = tb * 512, min(512, T - tb * 512)
                holder = {}

                def ft(e, mj=mj, mlen=mlen, t0=t0, tl=tl, holder=holder, wl=wl, bg=bg, bu=bu):
                    e.need("dpe", wl)
                    b, p = banks.get(e)
                    e.tick(mm(e, p, bg, 128, 0, mlen, hh_bf, T, t0, tl,
                              [(k, k, 128) for k in range(nf)]))
                    b2, p2 = banks.get(e)
                    e.tick(mm(e, p2, bu, 128, 0, mlen, hh_bf, T, t0, tl,
                              [(k, k, 128) for k in range(nf)]))
                    holder["b"] = (b, p, b2, p2)
                pr.sec("tensor", ft)

                def fs(e, li=li, mlen=mlen, t0=t0, tl=tl, holder=holder):
                    b, p, b2, p2 = holder["b"]
                    c = e.tick(e.g.activation(hidh[0:mlen, li * T + t0:li * T + t0 + tl],
                                              p[0:mlen, 0:tl], AF.Silu))
                    banks.release(b, "se", c)
                pr.sec("scalar", fs)

                def fv(e, li=li, mlen=mlen, t0=t0, tl=tl, holder=holder):
                    b, p, b2, p2 = holder["b"]
                    hv = hidh[0:mlen, li * T + t0:li * T + t0 + tl]
                    c = e.tick(e.g.tensor_mul(hv, hv, p2[0:mlen, 0:tl]))
                    banks.release(b2, "ve", c)
                pr.sec("vector", fv)
            wdep[ig] = ("te", pr.cnt["te"])
            wdep[iu] = ("te", pr.cnt["te"])

        for mj in range(nf):
            bw = wstr[mj % 2]
            iw = mj % 2

            def fldw2(e, mj=mj, half=half, bw=bw, iw=iw):
                if wdep[iw] is not None:
                    e.need(*wdep[iw])
                else:
                    e.waitall("te")
                for li in range(11):
                    kj = half * 11 + li
                    kr = min(128, 2752 - kj * 128)
                    e.dma(bw[0:kr, li * 128:li * 128 + 128],
                          wd_d[kj * 128:kj * 128 + kr, mj * 128:(mj + 1) * 128])
            pr.sec("gpsimd", fldw2, auto=False)
            wl2 = pr.cnt["dpe"]

            for tb in range(ntb):
                t0, tl = tb * 512, min(512, T - tb * 512)
                holder = {}

                def ft(e, mj=mj, half=half, t0=t0, tl=tl, holder=holder, wl2=wl2, bw=bw):
                    e.need("dpe", wl2)
                    b, p = banks.get(e)
                    ks = [(k, k, min(128, 2752 - (half * 11 + k) * 128)) for k in range(11)]
                    e.tick(mm(e, p, bw, 128, 0, 128, hidh, T, t0, tl, ks))
                    holder["b"] = (b, p)
                pr.sec("tensor", ft)

                def fv(e, mj=mj, t0=t0, tl=tl, holder=holder):
                    b, p = holder["b"]
                    av = x_sb[:, mj * T + t0:mj * T + t0 + tl]
                    c = e.tick(e.g.tensor_add(av, av, p[:, 0:tl]))
                    banks.release(b, "ve", c)
                pr.sec("vector", fv)
            wdep[iw] = ("te", pr.cnt["te"])

    for kj in range(nf):
        def fldo(e, kj=kj):
            e.dma(stage[:, 0:T], xo_outer_d[kj * 128:(kj + 1) * 128, :])
        pr.sec("sync", fldo)

        def fvo(e, kj=kj):
            e.tick(e.g.tensor_add(stage[:, 0:T], stage[:, 0:T], x_sb[:, kj * T:(kj + 1) * T]))
        pr.sec("vector", fvo)

        def fso(e, kj=kj):
            e.dma(xo_d[kj * 128:(kj + 1) * 128, :], stage[:, 0:T])
        pr.sec("sync", fso)

    def fend(e):
        e.need("dsp")
    pr.sec("sync", fend)
    return pr




def build_s4a(T=8192):
    pr = Prog("s4a")
    nf = 8
    TWO_PI = 6.283185307179586
    x_d = pr.din("x", [D_MODEL, T])
    ones_d = pr.din("ones_bf", [128, 128], BF16)
    iota_d = pr.din("iota_row", [1, T])
    BT_d = pr.din("BT_sl", [D_MODEL, 16])
    lar_d = pr.din("lar_sl", [16])
    ai_d = pr.din("ai_sl", [16])
    lstep_d = pr.din("log_step16", [16])
    xsr_d = pr.dout("xsr", [16, T])
    xsi_d = pr.dout("xsi", [16, T])

    banks = Banks(pr, 6)
    TC = 512
    ntc = T // TC
    x_sb = pr.sbuf("x", [128, nf * TC])
    u_bf = pr.sbuf("ub", [128, nf * TC], BF16)
    sq_bf = pr.sbuf("sqb", [128, nf * TC], BF16)
    mrow_s = pr.sbuf("mrs", [128, TC + 512])
    rs_s = pr.sbuf("rss", [128, TC])
    BT_bf = pr.sbuf("BT", [128, nf * 16], BF16)
    TA = pr.sbuf("TA", [64, T])
    TB = pr.sbuf("TB", [64, T])
    TCt = pr.sbuf("TCt", [64, T])
    TS = pr.sbuf("TS", [64, T])
    TD = pr.sbuf("TD", [64, T], BF16)
    TDb = pr.sbuf("TDb", [64, T], BF16)
    sc = pr.sbuf("scl", [16, 40])
    scb = pr.sbuf("scb", [64, 4])
    ones_bf = pr.sbuf("ones", [128, 128], BF16)
    LAR, AI, LST, STP, DR, TH, THR, RHO, SI, CO, AR, AIC, DEN, FR, FI, T1 = (
        0, 4, 8, 12, 16, 20, 24, 28, 32, 36, 1, 5, 9, 13, 17, 21)

    def fldsp(e):
        e.dma(ones_bf[:, :], ones_d)
        for r in range(16):
            e.dma(TA[32 + r:33 + r, 0:T], iota_d)
        e.dma(sc[0:16, LAR:LAR + 1], lar_d.rearrange("(p a) -> p a", a=1))
        e.dma(sc[0:16, AI:AI + 1], ai_d.rearrange("(p a) -> p a", a=1))
        e.dma(sc[0:16, LST:LST + 1], lstep_d.rearrange("(p a) -> p a", a=1))
    pr.sec("sync", fldsp)

    def fldbt(e):
        for kj in range(nf):
            e.dma(BT_bf[:, kj * 16:(kj + 1) * 16], BT_d[kj * 128:(kj + 1) * 128, :])
    pr.sec("gpsimd", fldbt)

    if S4STOP <= 1:
        return pr

    def s1(e):
        e.fence(e.g.activation(sc[0:16, STP:STP + 1], sc[0:16, LST:LST + 1], AF.Exp))
        e.fence(e.g.activation(sc[0:16, T1:T1 + 1], sc[0:16, LAR:LAR + 1], AF.Exp))
    pr.sec("scalar", s1)

    def v1(e):
        e.fence(e.g.tensor_mul(sc[0:16, DR:DR + 1], sc[0:16, T1:T1 + 1], sc[0:16, STP:STP + 1]))
        e.fence(e.g.tensor_scalar_mul(sc[0:16, DR:DR + 1], sc[0:16, DR:DR + 1], -1.0))
        e.fence(e.g.tensor_mul(sc[0:16, TH:TH + 1], sc[0:16, AI:AI + 1], sc[0:16, STP:STP + 1]))
        e.fence(e.g.tensor_scalar_mul(sc[0:16, THR:THR + 1], sc[0:16, TH:TH + 1], 1.0 / TWO_PI))
        e.fence(e.g.tensor_scalar_add(sc[0:16, AR:AR + 1], sc[0:16, THR:THR + 1], 12582912.0))
        e.fence(e.g.tensor_scalar_add(sc[0:16, AR:AR + 1], sc[0:16, AR:AR + 1], -12582912.0))
        e.fence(e.g.tensor_sub(sc[0:16, THR:THR + 1], sc[0:16, THR:THR + 1], sc[0:16, AR:AR + 1]))
        e.fence(e.g.tensor_scalar_mul(sc[0:16, THR:THR + 1], sc[0:16, THR:THR + 1], TWO_PI))
    pr.sec("vector", v1)

    def s2(e):
        e.fence(e.g.activation(sc[0:16, RHO:RHO + 1], sc[0:16, DR:DR + 1], AF.Exp))
        e.fence(e.g.activation(sc[0:16, SI:SI + 1], sc[0:16, THR:THR + 1], AF.Sin))
    pr.sec("scalar", s2)

    if S4STOP <= 2:
        return pr

    pic = pr.ccol(np.pi)

    def v2(e):
        e.fence(e.g.tensor_scalar_add(sc[0:16, CO:CO + 1], sc[0:16, THR:THR + 1],
                                      np.pi / 2))
        e.fence(e.g.tensor_single_scalar(sc[0:16, T1 + 8:T1 + 9], sc[0:16, CO:CO + 1],
                                         pic[0:16, 0:1], ALU.is_ge))
        e.fence(e.g.tensor_scalar_mul(sc[0:16, T1 + 8:T1 + 9],
                                      sc[0:16, T1 + 8:T1 + 9], TWO_PI))
        e.fence(e.g.tensor_sub(sc[0:16, CO:CO + 1], sc[0:16, CO:CO + 1],
                               sc[0:16, T1 + 8:T1 + 9]))
    pr.sec("vector", v2)

    def s3(e):
        e.fence(e.g.activation(sc[0:16, CO:CO + 1], sc[0:16, CO:CO + 1], AF.Sin))
    pr.sec("scalar", s3)

    def v3(e):
        e.fence(e.g.tensor_mul(sc[0:16, AR:AR + 1], sc[0:16, RHO:RHO + 1], sc[0:16, CO:CO + 1]))
        e.fence(e.g.tensor_mul(sc[0:16, AIC:AIC + 1], sc[0:16, RHO:RHO + 1], sc[0:16, SI:SI + 1]))
        e.fence(e.g.tensor_scalar_add(sc[0:16, AR:AR + 1], sc[0:16, AR:AR + 1], -1.0))
        e.fence(e.g.tensor_mul(sc[0:16, DEN:DEN + 1], sc[0:16, T1:T1 + 1], sc[0:16, T1:T1 + 1]))
        e.fence(e.g.tensor_mul(sc[0:16, FR:FR + 1], sc[0:16, AI:AI + 1], sc[0:16, AI:AI + 1]))
        e.fence(e.g.tensor_add(sc[0:16, DEN:DEN + 1], sc[0:16, DEN:DEN + 1], sc[0:16, FR:FR + 1]))
        e.fence(e.g.reciprocal(sc[0:16, DEN:DEN + 1], sc[0:16, DEN:DEN + 1]))
        e.fence(e.g.tensor_mul(sc[0:16, FR:FR + 1], sc[0:16, AR:AR + 1], sc[0:16, T1:T1 + 1]))
        e.fence(e.g.tensor_scalar_mul(sc[0:16, FR:FR + 1], sc[0:16, FR:FR + 1], -1.0))
        e.fence(e.g.tensor_mul(sc[0:16, T1 + 8:T1 + 9], sc[0:16, AIC:AIC + 1], sc[0:16, AI:AI + 1]))
        e.fence(e.g.tensor_add(sc[0:16, FR:FR + 1], sc[0:16, FR:FR + 1], sc[0:16, T1 + 8:T1 + 9]))
        e.fence(e.g.tensor_mul(sc[0:16, FR:FR + 1], sc[0:16, FR:FR + 1], sc[0:16, DEN:DEN + 1]))
    pr.sec("vector", v3)
    def v3b(e):
        e.fence(e.g.tensor_mul(sc[0:16, FI:FI + 1], sc[0:16, AIC:AIC + 1], sc[0:16, T1:T1 + 1]))
        e.fence(e.g.tensor_scalar_mul(sc[0:16, FI:FI + 1], sc[0:16, FI:FI + 1], -1.0))
        e.fence(e.g.tensor_mul(sc[0:16, T1 + 8:T1 + 9], sc[0:16, AR:AR + 1], sc[0:16, AI:AI + 1]))
        e.fence(e.g.tensor_sub(sc[0:16, FI:FI + 1], sc[0:16, FI:FI + 1], sc[0:16, T1 + 8:T1 + 9]))
        e.fence(e.g.tensor_mul(sc[0:16, FI:FI + 1], sc[0:16, FI:FI + 1], sc[0:16, DEN:DEN + 1]))
    pr.sec("vector", v3b)

    if S4STOP <= 3:
        return pr

    def vdup(e):
        e.fence(e.g.tensor_copy(scb[32:48, 0:1], sc[0:16, THR:THR + 1]))
        e.fence(e.g.tensor_copy(scb[32:48, 1:2], sc[0:16, RHO:RHO + 1]))
    pr.sec("vector", vdup)

    def vph(e):
        e.fence(e.g.tensor_scalar_mul(TB[32:48, :], TA[32:48, 0:T], scb[32:48, 0:1]))
        e.fence(e.g.tensor_scalar_mul(TB[32:48, :], TB[32:48, :], 1.0 / TWO_PI))
        e.fence(e.g.tensor_scalar_add(TCt[32:48, :], TB[32:48, :], 12582912.0))
        e.fence(e.g.tensor_scalar_add(TCt[32:48, :], TCt[32:48, :], -12582912.0))
        e.fence(e.g.tensor_sub(TB[32:48, 0:T], TB[32:48, 0:T], TCt[32:48, 0:T]))
        e.fence(e.g.tensor_scalar_mul(TB[32:48, 0:T], TB[32:48, 0:T], TWO_PI))
        e.fence(e.g.tensor_scalar_add(TB[0:16, 0:T], TB[32:48, 0:T], np.pi / 2))
        e.fence(e.g.tensor_single_scalar(TCt[0:16, 0:T], TB[0:16, 0:T],
                                         pic[0:16, 0:1], ALU.is_ge))
        e.fence(e.g.tensor_scalar_mul(TCt[0:16, 0:T], TCt[0:16, 0:T], TWO_PI))
        e.fence(e.g.tensor_sub(TB[0:16, 0:T], TB[0:16, 0:T], TCt[0:16, 0:T]))
    pr.sec("vector", vph)

    def sph(e):
        e.fence(e.g.activation(TD[32:48, 0:T], TB[32:48, 0:T], AF.Sin))
        e.fence(e.g.activation(TDb[0:16, 0:T], TB[32:48, 0:T], AF.Sin))
        e.fence(e.g.activation(TD[0:16, 0:T], TB[0:16, 0:T], AF.Sin))
        e.fence(e.g.activation(TDb[32:48, 0:T], TB[0:16, 0:T], AF.Sin))
    pr.sec("scalar", sph)

    if S4STOP <= 4:
        return pr

    for tc in range(ntc):
        def fldx(e, tc=tc):
            for kj in range(nf):
                e.dma(x_sb[:, kj * TC:(kj + 1) * TC],
                      x_d[kj * 128:(kj + 1) * 128, tc * TC:(tc + 1) * TC])
        pr.sec("sync", fldx)

        def fcast(e):
            e.tick(e.g.tensor_copy(u_bf[:, :], x_sb[:, 0:nf * TC]))
        pr.sec("gpsimd", fcast)
        emit_ln(pr, banks, x_sb, nf, TC, ones_bf, u_bf, mode="ln",
                name=f"s4ln{tc}", x_bf=u_bf, sq_bf=sq_bf,
                stats_bufs=(mrow_s, rs_s))
        for tb in range(TC // 512):
            t0 = tb * 512
            holder = {}

            def ft(e, tc=tc, t0=t0, holder=holder):
                b, p = banks.get(e)
                ins = None
                for kj in range(nf):
                    ins = e.g.matmul(p[0:16, 0:512], BT_bf[:, kj * 16:(kj + 1) * 16],
                                     u_bf[:, kj * TC + t0:kj * TC + t0 + 512],
                                     start=(kj == 0), stop=(kj == nf - 1))
                e.tick(ins)
                holder["b"] = (b, p)
            pr.sec("tensor", ft)

            def fv(e, tc=tc, t0=t0, holder=holder):
                b, p = holder["b"]
                c = e.tick(e.g.tensor_copy(TA[0:16, tc * TC + t0:tc * TC + t0 + 512],
                                           p[0:16, 0:512]))
                banks.release(b, "ve", c)
            pr.sec("vector", fv)

    if S4STOP <= 5:
        return pr

    def vbu(e):
        e.fence(e.g.tensor_scalar_mul(TB[0:16, 0:T], TA[0:16, 0:T], sc[0:16, FR:FR + 1]))
        e.fence(e.g.tensor_scalar_mul(TB[32:48, 0:T], TA[0:16, 0:T], sc[0:16, FI:FI + 1]))
        e.fence(e.g.tensor_mul(TCt[0:16, 0:T], TD[0:16, 0:T], TB[0:16, 0:T]))
        e.fence(e.g.tensor_mul(TA[0:16, 0:T], TD[32:48, 0:T], TB[32:48, 0:T]))
        e.fence(e.g.tensor_add(TCt[0:16, 0:T], TCt[0:16, 0:T], TA[0:16, 0:T]))
        e.fence(e.g.tensor_mul(TCt[32:48, 0:T], TDb[32:48, 0:T], TB[32:48, 0:T]))
        e.fence(e.g.tensor_mul(TA[32:48, 0:T], TDb[0:16, 0:T], TB[0:16, 0:T]))
        e.fence(e.g.tensor_sub(TCt[32:48, 0:T], TCt[32:48, 0:T], TA[32:48, 0:T]))
    pr.sec("vector", vbu)

    if S4STOP <= 6:
        return pr

    def vsc1(e):
        e.fence(e.g.tensor_tensor_scan(TS[0:16, 0:T], sc[0:16, RHO:RHO + 1].broadcast_to([16, T]),
                                       TCt[0:16, 0:T], 0.0, ALU.mult, ALU.add))
    pr.sec("vector", vsc1)

    if S4STOP <= 7:
        return pr

    def vsc2(e):
        e.fence(e.g.tensor_tensor_scan(TS[32:48, 0:T], scb[32:48, 1:2].broadcast_to([16, T]),
                                       TCt[32:48, 0:T], 0.0, ALU.mult, ALU.add))
    pr.sec("vector", vsc2)

    if S4STOP <= 8:
        return pr

    def vxs(e):
        e.fence(e.g.tensor_mul(TB[0:16, 0:T], TD[0:16, 0:T], TS[0:16, 0:T]))
        e.fence(e.g.tensor_mul(TA[0:16, 0:T], TD[32:48, 0:T], TS[32:48, 0:T]))
        e.fence(e.g.tensor_sub(TB[0:16, 0:T], TB[0:16, 0:T], TA[0:16, 0:T]))
        e.fence(e.g.tensor_mul(TB[32:48, 0:T], TDb[0:16, 0:T], TS[0:16, 0:T]))
        e.fence(e.g.tensor_mul(TA[32:48, 0:T], TDb[32:48, 0:T], TS[32:48, 0:T]))
        e.fence(e.g.tensor_add(TB[32:48, 0:T], TB[32:48, 0:T], TA[32:48, 0:T]))
    pr.sec("vector", vxs)

    if S4STOP <= 9:
        return pr

    def fst(e):
        e.dma(xsr_d, TB[0:16, 0:T])
        e.dma(xsi_d, TB[32:48, 0:T])
        e.need("dsp")
    pr.sec("sync", fst)
    return pr



def build_s4b(T=2048):
    pr = Prog("s4b")
    nf = 8
    x_d = pr.din("x", [D_MODEL, T])
    xsr_dd = pr.din("xsr", [64, T])
    xsi_dd = pr.din("xsi", [64, T])
    ones_d = pr.din("ones_bf", [128, 128], BF16)
    CrT_d = pr.din("CrT", [64, D_MODEL])
    CiT_d = pr.din("CiT", [64, D_MODEL])
    Dv_d = pr.din("Dv", [D_MODEL])
    xo_d = pr.dout("xo", [D_MODEL, T])

    banks = Banks(pr, 6)
    x_sb = pr.sbuf("x", [128, nf * T])
    u_bf = pr.sbuf("ub", [128, nf * T], BF16)
    sq_bf = pr.sbuf("sqb", [128, nf * T], BF16)
    xs_bf = pr.sbuf("xsb", [64, 2 * T], BF16)
    CrT_sb = pr.sbuf("crt", [64, D_MODEL], BF16)
    CiT_sb = pr.sbuf("cit", [64, D_MODEL], BF16)
    D_c = pr.sbuf("dc", [128, nf])
    ones_bf = pr.sbuf("ones", [128, 128], BF16)

    def fldsp(e):
        for kj in range(nf):
            e.dma(x_sb[:, kj * T:(kj + 1) * T], x_d[kj * 128:(kj + 1) * 128, :])
        e.dma(ones_bf[:, :], ones_d)
    pr.sec("sync", fldsp)

    def fldpe(e):
        e.dma(xs_bf[0:64, 0:T], xsr_dd)
        e.dma(xs_bf[0:64, T:2 * T], xsi_dd)
        e.dma(CrT_sb[0:64, :], CrT_d)
        e.dma(CiT_sb[0:64, :], CiT_d)
        for kj in range(nf):
            e.dma(D_c[:, kj:kj + 1], Dv_d.rearrange("(k p) -> k p", p=128)[kj].rearrange("(p a) -> p a", a=1))
    pr.sec("gpsimd", fldpe)

    def fneg(e):
        e.fence(e.g.tensor_scalar_mul(CiT_sb[0:64, :], CiT_sb[0:64, :], -1.0))
    pr.sec("vector", fneg)

    def fcast(e):
        e.tick(e.g.tensor_copy(u_bf[:, :], x_sb[:, 0:nf * T]))
    pr.sec("gpsimd", fcast)
    emit_ln(pr, banks, x_sb, nf, T, ones_bf, u_bf, mode="ln",
            name="s4bln", x_bf=u_bf, sq_bf=sq_bf)

    ntb = cdiv(T, 512)
    for mj in range(nf):
        for tb in range(ntb):
            t0, tl = tb * 512, min(512, T - tb * 512)
            holder = {}

            def ft(e, mj=mj, t0=t0, tl=tl, holder=holder):
                b, p = banks.get(e)
                e.g.matmul(p[:, 0:tl], CrT_sb[0:64, mj * 128:(mj + 1) * 128],
                           xs_bf[0:64, t0:t0 + tl], start=True, stop=False)
                e.tick(e.g.matmul(p[:, 0:tl], CiT_sb[0:64, mj * 128:(mj + 1) * 128],
                                  xs_bf[0:64, T + t0:T + t0 + tl], start=False, stop=True))
                holder["b"] = (b, p)
            pr.sec("tensor", ft)

            def fv(e, mj=mj, t0=t0, tl=tl, holder=holder):
                b, p = holder["b"]
                xv = x_sb[:, mj * T + t0:mj * T + t0 + tl]
                e.fence(e.g.tensor_add(xv, xv, p[:, 0:tl]))
                e.fence(e.g.scalar_tensor_tensor(
                    xv, u_bf[:, mj * T + t0:mj * T + t0 + tl], D_c[:, mj:mj + 1],
                    xv, ALU.mult, ALU.add))
                c = pr.cnt["ve"]
                banks.release(b, "ve", c)
            pr.sec("vector", fv)

    def fst(e):
        e.waitall("ve")
        for kj in range(nf):
            e.dma(xo_d[kj * 128:(kj + 1) * 128, :], x_sb[:, kj * T:(kj + 1) * T])
        e.need("dsp")
    pr.sec("sync", fst)
    return pr



def build_m1pre(T=2048, first=False):
    pr = Prog("m1pre")
    nf = 8
    T3 = T + 4
    xp_d = pr.din("xp", [D_MODEL, T3])
    ones_d = pr.din("ones_bf", [128, 128], BF16)
    win_d = pr.din("w_in", [D_MODEL, 4096])
    cw_d = pr.din("conv_w", [2048, 4])
    cb_d = pr.din("conv_b", [2048])
    wx_d = pr.din("w_x", [2048, 96])
    wdt_d = pr.din("w_dt", [64, 2048])
    bdt_d = pr.din("b_dt", [2048])
    xin_o = pr.dout("xin", [2048, T], BF16)
    zs_o = pr.dout("zs", [2048, T], BF16)
    del_o = pr.dout("delta", [2048, T])
    bc_o = pr.dout("bc", [32, T])

    banks = Banks(pr, 6)
    xp_sb = pr.sbuf("xp", [128, nf * T3])
    xpb = pr.sbuf("xpb", [128, nf * T3], BF16)
    sqs = pr.sbuf("sqs", [128, nf * T3], BF16) if first else None
    wstr = [pr.sbuf("wstr", [128, 8 * 128], BF16)]
    wdep = [None, None]
    wx_sb = pr.sbuf("wx", [128, 16 * 96], BF16)
    wdt_sb = pr.sbuf("wdt", [64, 2048], BF16)
    cw_c = pr.sbuf("cwc", [128, 16 * 4])
    cb_c = pr.sbuf("cbc", [128, 16])
    bdt_c = pr.sbuf("bdtc", [128, 16])
    ones_bf = pr.sbuf("ones", [128, 128], BF16)
    chnk = pr.sbuf("chnk", [128, T3], BF16)
    cacc = pr.sbuf("cacc", [128, T3])
    stage_bf = pr.sbuf("stgb", [128, T], BF16)
    xdbl = pr.sbuf("xdbl", [96, T])
    dtb = pr.sbuf("dtb", [64, T], BF16)
    xin_keep = pr.sbuf("xink", [128, 16 * T], BF16)

    def fldsp(e):
        for kj in range(nf):
            e.dma(xp_sb[:, kj * T3:(kj + 1) * T3], xp_d[kj * 128:(kj + 1) * 128, :])
        e.dma(ones_bf[:, :], ones_d)
    pr.sec("sync", fldsp)

    def fldpe(e):
        for kj in range(16):
            e.dma(wx_sb[:, kj * 96:(kj + 1) * 96], wx_d[kj * 128:(kj + 1) * 128, :])
        e.dma(wdt_sb[0:64, :], wdt_d)
        for kj in range(16):
            e.dma(cw_c[:, kj * 4:kj * 4 + 4],
                  cw_d.rearrange("(k p) t -> k p t", p=128)[kj])
            e.dma(cb_c[:, kj:kj + 1], cb_d.rearrange("(k p) -> k p", p=128)[kj].rearrange("(p a) -> p a", a=1))
            e.dma(bdt_c[:, kj:kj + 1], bdt_d.rearrange("(k p) -> k p", p=128)[kj].rearrange("(p a) -> p a", a=1))
    pr.sec("gpsimd", fldpe)

    if first:
        def fcast(e):
            e.tick(e.g.tensor_copy(xpb[:, :], xp_sb[:, 0:nf * T3]))
        pr.sec("gpsimd", fcast)
        emit_ln(pr, banks, xp_sb, nf, T3, ones_bf, xpb, mode="ln",
                name="m1ln", x_bf=xpb, sq_bf=sqs)
    else:
        def fcast(e):
            e.tick(e.g.tensor_copy(xpb[:, :], xp_sb[:, 0:nf * T3]))
        pr.sec("gpsimd", fcast)

    ntb3 = cdiv(T3, 512)
    ntb = cdiv(T, 512)
    for mj in range(32):
        is_x = mj < 16
        iw = mj % len(wstr)
        bw = wstr[iw]

        def fldw(e, mj=mj, bw=bw, iw=iw):
            if wdep[iw] is not None:
                e.need(*wdep[iw])
            else:
                e.waitall("te")
            for kj in range(nf):
                e.dma(bw[:, kj * 128:(kj + 1) * 128],
                      win_d[kj * 128:(kj + 1) * 128, mj * 128:(mj + 1) * 128])
        pr.sec("gpsimd", fldw, auto=False)
        wl = pr.cnt["dpe"]

        nb = ntb3 if is_x else ntb
        for tb in range(nb):
            t0 = tb * 512
            tl = min(512, (T3 if is_x else T) - t0)
            holder = {}

            def ft(e, mj=mj, t0=t0, tl=tl, holder=holder, wl=wl, bw=bw, is_x=is_x):
                e.need("dpe", wl)
                b, p = banks.get(e)
                toff = t0 if is_x else t0 + 4
                e.tick(mm(e, p, bw, 128, 0, 128, xpb, T3, toff, tl,
                          [(k, k, 128) for k in range(nf)]))
                holder["b"] = (b, p)
            pr.sec("tensor", ft)

            if is_x:
                def fv(e, mj=mj, t0=t0, tl=tl, holder=holder):
                    b, p = holder["b"]
                    c = e.tick(e.g.tensor_copy(chnk[:, t0:t0 + tl], p[:, 0:tl]))
                    banks.release(b, "ve", c)
                pr.sec("vector", fv)
            else:
                def fs(e, mj=mj, t0=t0, tl=tl, holder=holder):
                    b, p = holder["b"]
                    c = e.tick(e.g.activation(stage_bf[:, t0:t0 + tl], p[:, 0:tl], AF.Silu))
                    banks.release(b, "se", c)
                pr.sec("scalar", fs)
        if is_x:
            def fcv2(e, mj=mj):
                e.fence(e.g.tensor_scalar_mul(cacc[:, 0:T], chnk[:, 1:1 + T],
                                              cw_c[:, mj * 4 + 0:mj * 4 + 1]))
                for tap in range(1, 4):
                    e.fence(e.g.scalar_tensor_tensor(
                        cacc[:, 0:T], chnk[:, 1 + tap:1 + tap + T],
                        cw_c[:, mj * 4 + tap:mj * 4 + tap + 1],
                        cacc[:, 0:T], ALU.mult, ALU.add))
                e.fence(e.g.tensor_scalar_add(cacc[:, 0:T], cacc[:, 0:T],
                                              cb_c[:, mj:mj + 1]))
            pr.sec("vector", fcv2)

            def fsl(e, mj=mj):
                e.tick(e.g.activation(xin_keep[:, mj * T:(mj + 1) * T], cacc[:, 0:T],
                                      AF.Silu))
            pr.sec("scalar", fsl)

            def fso(e, mj=mj):
                e.dma(xin_o[mj * 128:(mj + 1) * 128, :], xin_keep[:, mj * T:(mj + 1) * T])
            pr.sec("sync", fso)
        else:
            def fso(e, mj=mj):
                e.dma(zs_o[(mj - 16) * 128:(mj - 15) * 128, :], stage_bf[:, 0:T])
            pr.sec("sync", fso)
        wdep[iw] = ("te", pr.cnt["te"])

    for tb in range(ntb):
        t0, tl = tb * 512, min(512, T - tb * 512)
        holder = {}

        def ft(e, t0=t0, tl=tl, holder=holder):
            b, p = banks.get(e)
            ins = None
            for kj in range(16):
                ins = e.g.matmul(p[0:96, 0:tl], wx_sb[:, kj * 96:(kj + 1) * 96],
                                 xin_keep[:, kj * T + t0:kj * T + t0 + tl],
                                 start=(kj == 0), stop=(kj == 15))
            e.tick(ins)
            holder["b"] = (b, p)
        pr.sec("tensor", ft)

        def fv(e, t0=t0, tl=tl, holder=holder):
            b, p = holder["b"]
            c = e.tick(e.g.tensor_copy(xdbl[0:96, t0:t0 + tl], p[0:96, 0:tl]))
            banks.release(b, "ve", c)
        pr.sec("vector", fv)

    def fvdt(e):
        e.tick(e.g.tensor_copy(dtb[0:64, :], xdbl[0:64, 0:T]))
    pr.sec("vector", fvdt)

    def fsbc(e):
        e.dma(bc_o, xdbl.rearrange("p t -> p t")[64:96, 0:T])
    pr.sec("sync", fsbc)

    spa = pr.sbuf("spa", [128, T])
    for mj in range(16):
        for tb in range(ntb):
            t0, tl = tb * 512, min(512, T - tb * 512)
            holder = {}

            def ft(e, mj=mj, t0=t0, tl=tl, holder=holder):
                b, p = banks.get(e)
                e.tick(e.g.matmul(p[:, 0:tl], wdt_sb[0:64, mj * 128:(mj + 1) * 128],
                                  dtb[0:64, t0:t0 + tl], start=True, stop=True))
                holder["b"] = (b, p)
            pr.sec("tensor", ft)

            def fv(e, mj=mj, t0=t0, tl=tl, holder=holder):
                b, p = holder["b"]
                c = e.tick(e.g.tensor_scalar_add(cacc[:, t0:t0 + tl], p[:, 0:tl],
                                                 bdt_c[:, mj:mj + 1]))
                banks.release(b, "ve", c)
            pr.sec("vector", fv)

        def fsp1(e, mj=mj):
            e.fence(e.g.activation(spa[:, 0:T], cacc[:, 0:T], AF.Abs))
            e.fence(e.g.activation(spa[:, 0:T], spa[:, 0:T], AF.Exp, scale=-1.0))
            e.fence(e.g.activation(spa[:, 0:T], spa[:, 0:T], AF.Ln, bias=1.0))
        pr.sec("scalar", fsp1)

        def fsp2(e, mj=mj):
            e.fence(e.g.tensor_scalar_max(cacc[:, 0:T], cacc[:, 0:T], 0.0))
            e.tick(e.g.tensor_add(cacc[:, 0:T], cacc[:, 0:T], spa[:, 0:T]))
        pr.sec("vector", fsp2)

        def fso2(e, mj=mj):
            e.dma(del_o[mj * 128:(mj + 1) * 128, :], cacc[:, 0:T])
        pr.sec("sync", fso2)

    def fend(e):
        e.need("dsp")
    pr.sec("sync", fend)
    return pr



def build_m1scan(T=8192):
    pr = Prog("m1scan")
    TH = T // 2
    del_d = pr.din("delta_sl", [256, T])
    xin_d = pr.din("xin_sl", [256, T], BF16)
    bc_d = pr.din("bc", [32, T])
    alog_d = pr.din("alog_sl", [256, 16])
    bc16_d = pr.din("bcast16", [16, 16 * 128], BF16)
    y_o = pr.dout("y_sl", [256, T])

    banks = Banks(pr, 4)
    delta = pr.sbuf("delta", [128, TH])
    xin = pr.sbuf("xin", [128, TH], BF16)
    bcb = pr.sbuf("bcb", [16, TH], BF16)
    ccb = pr.sbuf("ccb", [16, TH], BF16)
    acol = pr.sbuf("acol", [128, 2 * 16])
    a_t = pr.sbuf("a_t", [128, TH])
    w_t = pr.sbuf("w_t", [128, TH], BF16)
    h_t = pr.sbuf("h_t", [128, TH])
    y_t = pr.sbuf("y_t", [128, TH])
    brep = pr.sbuf("brep", [128, TH], BF16)
    crep = pr.sbuf("crep", [128, TH], BF16)
    bc16 = pr.sbuf("bc16", [16, 16 * 128], BF16)
    carry = pr.sbuf("carry", [128, 16])

    def fldpe(e):
        e.dma(bc16[0:16, :], bc16_d)
        for cc in range(2):
            e.dma(acol[:, cc * 16:(cc + 1) * 16], alog_d[cc * 128:(cc + 1) * 128, :])
    pr.sec("gpsimd", fldpe)

    def fA(e):
        e.fence(e.g.activation(acol[:, 0:32], acol[:, 0:32], AF.Exp))
    pr.sec("scalar", fA)

    def fA2(e):
        e.fence(e.g.tensor_scalar_mul(acol[:, 0:32], acol[:, 0:32], -1.0))
    pr.sec("vector", fA2)

    ntb = cdiv(TH, 512)
    for cc in range(2):
        for half in range(2):
            hof = half * TH

            def fldc(e, cc=cc, hof=hof):
                e.dma(delta[:, :], del_d[cc * 128:(cc + 1) * 128, hof:hof + TH])
                e.dma(xin[:, :], xin_d[cc * 128:(cc + 1) * 128, hof:hof + TH])
            pr.sec("sync", fldc)

            def fldb(e, hof=hof):
                e.dma(bcb[0:16, :], bc_d[0:16, hof:hof + TH])
                e.dma(ccb[0:16, :], bc_d[16:32, hof:hof + TH])
            pr.sec("gpsimd", fldb)

            def fz(e):
                e.tick(e.g.memset(y_t[:, :], 0.0))
            pr.sec("vector", fz)

            for n in range(16):
                for tb in range(ntb):
                    t0 = tb * 512
                    tl = min(512, TH - t0)
                    holder = {}

                    def ft(e, n=n, t0=t0, tl=tl, holder=holder):
                        b, p = banks.get(e)
                        e.g.matmul(p[:, 0:tl], bc16[0:16, n * 128:(n + 1) * 128],
                                   bcb[0:16, t0:t0 + tl], start=True, stop=True)
                        b2, p2 = banks.get(e)
                        e.tick(e.g.matmul(p2[:, 0:tl], bc16[0:16, n * 128:(n + 1) * 128],
                                          ccb[0:16, t0:t0 + tl], start=True, stop=True))
                        holder["b"] = (b, p, b2, p2)
                    pr.sec("tensor", ft)

                    def fv(e, n=n, t0=t0, tl=tl, holder=holder):
                        b, p, b2, p2 = holder["b"]
                        e.fence(e.g.tensor_copy(brep[:, t0:t0 + tl], p[:, 0:tl]))
                        c = e.fence(e.g.tensor_copy(crep[:, t0:t0 + tl], p2[:, 0:tl]))
                        banks.release(b, "ve", c)
                        banks.release(b2, "ve", c)
                    pr.sec("vector", fv)

                def fsc(e, cc=cc, n=n):
                    e.tick(e.g.activation(a_t[:, :], delta[:, :], AF.Exp,
                                          scale=acol[:, cc * 16 + n:cc * 16 + n + 1]))
                pr.sec("scalar", fsc)

                def fw(e, n=n):
                    e.fence(e.g.tensor_mul(w_t[:, :], delta[:, :], xin[:, :]))
                    e.fence(e.g.tensor_mul(w_t[:, :], w_t[:, :], brep[:, :]))
                pr.sec("vector", fw)

                def fsn(e, n=n, half=half):
                    init = 0.0 if half == 0 else carry[:, n:n + 1]
                    e.fence(e.g.tensor_tensor_scan(h_t[:, :], a_t[:, :], w_t[:, :],
                                                   init, ALU.mult, ALU.add))
                    e.fence(e.g.tensor_copy(carry[:, n:n + 1], h_t[:, TH - 1:TH]))
                    e.fence(e.g.tensor_mul(w_t[:, :], h_t[:, :], crep[:, :]))
                    e.fence(e.g.tensor_add(y_t[:, :], y_t[:, :], w_t[:, :]))
                pr.sec("vector", fsn)

            def fso(e, cc=cc, hof=hof):
                e.dma(y_o[cc * 128:(cc + 1) * 128, hof:hof + TH], y_t[:, :])
            pr.sec("sync", fso)

    def fend(e):
        e.need("dsp")
    pr.sec("sync", fend)
    return pr



def build_m1post(T=2048, last=False):
    pr = Prog("m1post")
    nf = 8
    xp_d = pr.din("xp", [D_MODEL, T])
    y_d = pr.din("y", [2048, T])
    xin_d = pr.din("xin", [2048, T], BF16)
    zs_d = pr.din("zs", [2048, T], BF16)
    dp_d = pr.din("dp", [2048])
    wout_d = pr.din("w_out", [2048, D_MODEL])
    if last:
        xouter_d = pr.din("x_outer", [D_MODEL, T])
    xo_d = pr.dout("xo", [D_MODEL, T])

    banks = Banks(pr, 6)
    xp_sb = pr.sbuf("xp", [128, nf * T])
    y2b = pr.sbuf("y2b", [128, 16 * T], BF16)
    wout_sb = pr.sbuf("wout", [128, 16 * 1024], BF16)
    dp_c = pr.sbuf("dpc", [128, 16])
    ych = pr.sbuf("ych", [128, T])
    xich = pr.sbuf("xich", [128, T], BF16)
    zsch = pr.sbuf("zsch", [128, T], BF16)
    stage = pr.sbuf("stage", [128, T])

    def fldsp(e):
        for kj in range(nf):
            e.dma(xp_sb[:, kj * T:(kj + 1) * T], xp_d[kj * 128:(kj + 1) * 128, :])
    pr.sec("sync", fldsp)

    def fldpe(e):
        for kj in range(16):
            e.dma(wout_sb[:, kj * 1024:(kj + 1) * 1024], wout_d[kj * 128:(kj + 1) * 128, :])
        for kj in range(16):
            e.dma(dp_c[:, kj:kj + 1], dp_d.rearrange("(k p) -> k p", p=128)[kj].rearrange("(p a) -> p a", a=1))
    pr.sec("gpsimd", fldpe)

    for kj in range(16):
        def fld(e, kj=kj):
            e.dma(ych[:, :], y_d[kj * 128:(kj + 1) * 128, :])
            e.dma(xich[:, :], xin_d[kj * 128:(kj + 1) * 128, :])
            e.dma(zsch[:, :], zs_d[kj * 128:(kj + 1) * 128, :])
        pr.sec("sync", fld)

        def fv(e, kj=kj):
            e.fence(e.g.scalar_tensor_tensor(ych[:, :], xich[:, :], dp_c[:, kj:kj + 1],
                                             ych[:, :], ALU.mult, ALU.add))
            e.tick(e.g.tensor_mul(y2b[:, kj * T:(kj + 1) * T], ych[:, :], zsch[:, :]))
        pr.sec("vector", fv)

    ntb = cdiv(T, 512)
    for mj in range(nf):
        for tb in range(ntb):
            t0, tl = tb * 512, min(512, T - tb * 512)
            holder = {}

            def ft(e, mj=mj, t0=t0, tl=tl, holder=holder):
                b, p = banks.get(e)
                e.tick(mm(e, p, wout_sb, 1024, mj * 128, 128, y2b, T, t0, tl,
                          [(k, k, 128) for k in range(16)]))
                holder["b"] = (b, p)
            pr.sec("tensor", ft)

            def fv(e, mj=mj, t0=t0, tl=tl, holder=holder):
                b, p = holder["b"]
                xv = xp_sb[:, mj * T + t0:mj * T + t0 + tl]
                c = e.tick(e.g.tensor_add(xv, xv, p[:, 0:tl]))
                banks.release(b, "ve", c)
            pr.sec("vector", fv)

    if last:
        for kj in range(nf):
            def fldo(e, kj=kj):
                e.dma(stage[:, :], xouter_d[kj * 128:(kj + 1) * 128, :])
            pr.sec("sync", fldo)

            def fvo(e, kj=kj):
                e.tick(e.g.tensor_add(xp_sb[:, kj * T:(kj + 1) * T],
                                      xp_sb[:, kj * T:(kj + 1) * T], stage[:, :]))
            pr.sec("vector", fvo)

    def fst(e):
        e.waitall("ve")
        for kj in range(nf):
            e.dma(xo_d[kj * 128:(kj + 1) * 128, :], xp_sb[:, kj * T:(kj + 1) * T])
        e.need("dsp")
    pr.sec("sync", fst)
    return pr



def build_attpre(T=2048):
    pr = Prog("attpre")
    nf = 8
    x_d = pr.din("x", [D_MODEL, T])
    ones_d = pr.din("ones_bf", [128, 128], BF16)
    wv_d = pr.din("wv", [D_MODEL, D_MODEL])
    bv_d = pr.din("bv", [D_MODEL])
    ln_o = pr.dout("ln_bf", [D_MODEL, T], BF16)
    vh_o = pr.dout("vh_bf", [D_MODEL, T], BF16)

    banks = Banks(pr, 6)
    x_sb = pr.sbuf("x", [128, nf * T])
    ln_bf = pr.sbuf("lnb", [128, nf * T], BF16)
    sqs = pr.sbuf("sqs", [128, nf * T], BF16)
    wv_sb = pr.sbuf("wv", [128, 8 * 1024], BF16)
    bv_c = pr.sbuf("bvc", [128, nf])
    ones_bf = pr.sbuf("ones", [128, 128], BF16)
    stage = pr.sbuf("stage", [128, 512], BF16)

    def fldsp(e):
        for kj in range(nf):
            e.dma(x_sb[:, kj * T:(kj + 1) * T], x_d[kj * 128:(kj + 1) * 128, :])
        e.dma(ones_bf[:, :], ones_d)
    pr.sec("sync", fldsp)

    def fldpe(e):
        for kj in range(nf):
            e.dma(wv_sb[:, kj * 1024:(kj + 1) * 1024], wv_d[kj * 128:(kj + 1) * 128, :])
            e.dma(bv_c[:, kj:kj + 1], bv_d.rearrange("(k p) -> k p", p=128)[kj].rearrange("(p a) -> p a", a=1))
    pr.sec("gpsimd", fldpe)

    def fcast(e):
        e.tick(e.g.tensor_copy(ln_bf[:, :], x_sb[:, 0:nf * T]))
    pr.sec("gpsimd", fcast)
    emit_ln(pr, banks, x_sb, nf, T, ones_bf, ln_bf, mode="ln",
            name="attln", x_bf=ln_bf, sq_bf=sqs)

    def fsol(e):
        e.waitall("ve")
        for kj in range(nf):
            e.dma(ln_o[kj * 128:(kj + 1) * 128, :], ln_bf[:, kj * T:(kj + 1) * T])
    pr.sec("sync", fsol)

    ntb = cdiv(T, 512)
    for mj in range(nf):
        for tb in range(ntb):
            t0, tl = tb * 512, min(512, T - tb * 512)
            holder = {}

            def ft(e, mj=mj, t0=t0, tl=tl, holder=holder):
                b, p = banks.get(e)
                e.tick(mm(e, p, wv_sb, 1024, mj * 128, 128, ln_bf, T, t0, tl,
                          [(k, k, 128) for k in range(nf)]))
                holder["b"] = (b, p)
            pr.sec("tensor", ft)

            def fv(e, mj=mj, t0=t0, tl=tl, holder=holder):
                b, p = holder["b"]
                c = e.tick(e.g.tensor_scalar_add(stage[:, 0:tl], p[:, 0:tl],
                                                 bv_c[:, mj:mj + 1]))
                banks.release(b, "ve", c)
            pr.sec("vector", fv)

            def fso(e, mj=mj, t0=t0, tl=tl):
                e.dma(vh_o[mj * 128:(mj + 1) * 128, t0:t0 + tl], stage[:, 0:tl])
            pr.sec("sync", fso)

    def fend(e):
        e.need("dsp")
    pr.sec("sync", fend)
    return pr



def build_att(T=8192, UNITS=4):
    pr = Prog("att")
    xh_d = pr.din("xh_sl", [UNITS * 64, T], BF16)
    vt_d = pr.din("vt_sl", [T, UNITS * 64], BF16)
    lt_d = pr.din("lt_sl", [UNITS * 64, 64])
    identbf_d = pr.din("ident_bf", [128, 128], BF16)
    o_d = pr.dout("o_sl", [T, UNITS * 64], BF16)

    banks = Banks(pr, 4)
    xh = pr.sbuf("xh", [64, UNITS * T], BF16)
    vt = pr.sbuf("vt", [128, (T // 128) * 64], BF16)
    lt = pr.sbuf("lt", [64, UNITS * 64], BF16)
    wqk = pr.sbuf("wqk", [64, 64], BF16)
    qf = pr.sbuf("qf", [64, T], BF16)
    sful = pr.sbuf("sful", [128, T])
    pbf = pr.sbuf("pbf", [128, T], BF16)
    ptb = pr.sbuf("ptb", [128, T], BF16)
    ident_bf = pr.sbuf("identbf", [128, 128], BF16)
    mxa = pr.sbuf("mxa", [128, 20])
    ob = pr.sbuf("ob", [128, 64], BF16)
    po = pr.psum("po", [128, 512], F32)
    pt2 = pr.psum("pt2", [128, 1024], BF16)

    def fldsp(e):
        e.dma(ident_bf[:, :], identbf_d)
        for u in range(UNITS):
            e.dma(xh[0:64, u * T:(u + 1) * T], xh_d[u * 64:(u + 1) * 64, :])
    pr.sec("sync", fldsp)

    def fldlt(e):
        e.dma(lt[0:64, :].rearrange("k (u i) -> k u i", i=64),
              lt_d.rearrange("(u k) i -> k u i", k=64))
    pr.sec("gpsimd", fldlt)

    for u in range(UNITS):
        def fldv(e, u=u):
            e.dma(vt[:, :].rearrange("p (n d) -> p n d", d=64),
                  vt_d[:, u * 64:(u + 1) * 64].rearrange("(n p) d -> p n d", p=128))
        pr.sec("sync", fldv)

        holder = {}

        def ftq(e, u=u, holder=holder):
            b, p = banks.get(e)
            e.tick(e.g.matmul(p[0:64, 0:64], lt[0:64, u * 64:(u + 1) * 64],
                              lt[0:64, u * 64:(u + 1) * 64], start=True, stop=True))
            holder["b"] = (b, p)
        pr.sec("tensor", ftq)

        def fvq(e, holder=holder):
            b, p = holder["b"]
            c = e.tick(e.g.tensor_copy(wqk[0:64, :], p[0:64, 0:64]))
            banks.release(b, "ve", c)
        pr.sec("vector", fvq)

        for tb in range(T // 512):
            t0 = tb * 512
            holder2 = {}

            def ftq2(e, u=u, t0=t0, holder2=holder2):
                b, p = banks.get(e)
                e.tick(e.g.matmul(p[0:64, 0:512], wqk[0:64, :],
                                  xh[0:64, u * T + t0:u * T + t0 + 512],
                                  start=True, stop=True))
                holder2["b"] = (b, p)
            pr.sec("tensor", ftq2)

            def fvq2(e, u=u, t0=t0, holder2=holder2):
                b, p = holder2["b"]
                c = e.tick(e.g.tensor_copy(qf[0:64, t0:t0 + 512], p[0:64, 0:512]))
                banks.release(b, "ve", c)
            pr.sec("vector", fvq2)

        nq = T // 128
        for qt in range(nq):
            for kb in range(T // 512):
                k0 = kb * 512
                holder3 = {}

                def fts(e, u=u, qt=qt, k0=k0, holder3=holder3):
                    b, p = banks.get(e)
                    e.tick(e.g.matmul(p[:, 0:512], qf[0:64, qt * 128:(qt + 1) * 128],
                                      xh[0:64, u * T + k0:u * T + k0 + 512],
                                      start=True, stop=True))
                    holder3["b"] = (b, p)
                pr.sec("tensor", fts)

                def fvs(e, k0=k0, kb=kb, holder3=holder3):
                    b, p = holder3["b"]
                    e.fence(e.g.tensor_copy(sful[:, k0:k0 + 512], p[:, 0:512]))
                    c = e.tick(e.g.reduce_max(mxa[:, kb:kb + 1], sful[:, k0:k0 + 512],
                                              axis=AX))
                    banks.release(b, "ve", c)
                pr.sec("vector", fvs)

            def fvm(e):
                e.fence(e.g.reduce_max(mxa[:, 16:17], mxa[:, 0:T // 512], axis=AX))
                e.fence(e.g.tensor_scalar_mul(mxa[:, 16:17], mxa[:, 16:17], -0.125))
            pr.sec("vector", fvm)

            def fse(e):
                e.tick(e.g.activation(pbf[:, 0:T], sful[:, 0:T], AF.Exp,
                                      bias=mxa[:, 16:17], scale=0.125))
            pr.sec("scalar", fse)

            def fvr(e):
                e.fence(e.g.reduce_sum(mxa[:, 17:18], pbf[:, 0:T], axis=AX))
                e.fence(e.g.reciprocal(mxa[:, 17:18], mxa[:, 17:18]))
            pr.sec("vector", fvr)

            for grp in range(T // 512):
                def ftt(e, grp=grp):
                    if grp > 0:
                        e.need("ve")
                    ins = None
                    for j in range(4):
                        kc = grp * 4 + j
                        ins = e.g.transpose(pt2[:, (j % 4) * 128:(j % 4) * 128 + 128],
                                           pbf[:, kc * 128:(kc + 1) * 128],
                                           ident_bf[:, :])
                    e.tick(ins)
                pr.sec("tensor", ftt, auto=(grp == 0))

                def fvt(e, grp=grp):
                    e.waitall("te")
                    c = e.tick(e.g.tensor_copy(ptb[:, grp * 512:(grp + 1) * 512],
                                               pt2[:, 0:512]))
                pr.sec("vector", fvt, auto=False)

            def fta(e, u=u, qt=qt):
                e.waitall("ve", "dsp")
                ins = None
                for kc in range(T // 128):
                    ins = e.g.matmul(po[:, 0:64], ptb[:, kc * 128:(kc + 1) * 128],
                                     vt[:, kc * 64:(kc + 1) * 64],
                                     start=(kc == 0), stop=(kc == T // 128 - 1))
                e.tick(ins)
            pr.sec("tensor", fta)

            def fvo(e, qt=qt):
                c = e.tick(e.g.tensor_scalar_mul(ob[:, 0:64], po[:, 0:64],
                                                 mxa[:, 17:18]))
            pr.sec("vector", fvo)

            def fsoo(e, u=u, qt=qt):
                e.dma(o_d[qt * 128:(qt + 1) * 128, u * 64:(u + 1) * 64], ob[:, 0:64])
            pr.sec("sync", fsoo)

    def fend(e):
        e.need("dsp")
    pr.sec("sync", fend)
    return pr



def build_attpost(T=2048):
    pr = Prog("attpost")
    nf = 8
    x_d = pr.din("x", [D_MODEL, T])
    of_d = pr.din("o_fm", [D_MODEL, T], BF16)
    wo_d = pr.din("wo", [D_MODEL, D_MODEL])
    bo_d = pr.din("bo", [D_MODEL])
    xo_d = pr.dout("xo", [D_MODEL, T])

    banks = Banks(pr, 6)
    x_sb = pr.sbuf("x", [128, nf * T])
    of_sb = pr.sbuf("of", [128, nf * T], BF16)
    wo_sb = pr.sbuf("wo", [128, 8 * 1024], BF16)
    bo_c = pr.sbuf("boc", [128, nf])

    def fldsp(e):
        for kj in range(nf):
            e.dma(x_sb[:, kj * T:(kj + 1) * T], x_d[kj * 128:(kj + 1) * 128, :])
            e.dma(of_sb[:, kj * T:(kj + 1) * T], of_d[kj * 128:(kj + 1) * 128, :])
    pr.sec("sync", fldsp)

    def fldpe(e):
        for kj in range(nf):
            e.dma(wo_sb[:, kj * 1024:(kj + 1) * 1024], wo_d[kj * 128:(kj + 1) * 128, :])
            e.dma(bo_c[:, kj:kj + 1], bo_d.rearrange("(k p) -> k p", p=128)[kj].rearrange("(p a) -> p a", a=1))
    pr.sec("gpsimd", fldpe)

    ntb = cdiv(T, 512)
    for mj in range(nf):
        for tb in range(ntb):
            t0, tl = tb * 512, min(512, T - tb * 512)
            holder = {}

            def ft(e, mj=mj, t0=t0, tl=tl, holder=holder):
                b, p = banks.get(e)
                e.tick(mm(e, p, wo_sb, 1024, mj * 128, 128, of_sb, T, t0, tl,
                          [(k, k, 128) for k in range(nf)]))
                holder["b"] = (b, p)
            pr.sec("tensor", ft)

            def fv(e, mj=mj, t0=t0, tl=tl, holder=holder):
                b, p = holder["b"]
                xv = x_sb[:, mj * T + t0:mj * T + t0 + tl]
                e.fence(e.g.tensor_add(xv, xv, p[:, 0:tl]))
                c = e.tick(e.g.tensor_scalar_add(xv, xv, bo_c[:, mj:mj + 1]))
                banks.release(b, "ve", c)
            pr.sec("vector", fv)

    def fst(e):
        e.waitall("ve")
        for kj in range(nf):
            e.dma(xo_d[kj * 128:(kj + 1) * 128, :], x_sb[:, kj * T:(kj + 1) * T])
        e.need("dsp")
    pr.sec("sync", fst)
    return pr



def build_m2pre(T=2048):
    pr = Prog("m2pre")
    nf = 8
    T3 = T + 3
    x_d = pr.din("x", [D_MODEL, T3])
    ones_d = pr.din("ones_bf", [128, 128], BF16)
    nw_d = pr.din("norm_w", [D_MODEL])
    win_d = pr.din("w_in", [D_MODEL, 4384])
    cw_d = pr.din("conv_w", [2304, 4])
    cb_d = pr.din("conv_b", [2304])
    dtb_d = pr.din("dt_bias", [32])
    h_o = pr.dout("h", [D_MODEL, T])
    zs_o = pr.dout("zs", [2048, T], BF16)
    xs_o = pr.dout("xs", [2048, T], BF16)
    b_o = pr.dout("bm", [128, T], BF16)
    c_o = pr.dout("cm", [128, T], BF16)
    dt_o = pr.dout("dt", [32, T])

    banks = Banks(pr, 6)
    x_sb = pr.sbuf("x", [128, nf * T3])
    u_bf = pr.sbuf("ub", [128, nf * T3], BF16)
    sqs = pr.sbuf("sqs", [128, nf * T3], BF16)
    wstr = [pr.sbuf("wstr", [128, nf * 128], BF16) for _ in range(2)]
    wdep = [None, None]
    ones_bf = pr.sbuf("ones", [128, 128], BF16)
    nw_c = pr.sbuf("nwc", [128, nf])
    cw_c = pr.sbuf("cwc", [128, 18 * 4])
    cb_c = pr.sbuf("cbc", [128, 18])
    dtb_c = pr.sbuf("dtbc", [32, 1])
    chnk = pr.sbuf("chnk", [128, T3], BF16)
    cacc = pr.sbuf("cacc", [128, T])
    stage_bf = pr.sbuf("stgb", [128, T], BF16)
    stage_f = pr.sbuf("stgf", [128, T])
    dtacc = pr.sbuf("dtacc", [32, T])
    spa = pr.sbuf("spa", [32, T])

    def fldsp(e):
        for kj in range(nf):
            e.dma(x_sb[:, kj * T3:(kj + 1) * T3], x_d[kj * 128:(kj + 1) * 128, :])
        e.dma(ones_bf[:, :], ones_d)
    pr.sec("sync", fldsp)

    def fldpe(e):
        for kj in range(nf):
            e.dma(nw_c[:, kj:kj + 1], nw_d.rearrange("(k p) -> k p", p=128)[kj].rearrange("(p a) -> p a", a=1))
        for ci in range(18):
            e.dma(cw_c[:, ci * 4:(ci + 1) * 4], cw_d[ci * 128:(ci + 1) * 128, :])
            e.dma(cb_c[:, ci:ci + 1], cb_d.rearrange("(k p) -> k p", p=128)[ci].rearrange("(p a) -> p a", a=1))
        e.dma(dtb_c[0:32, 0:1], dtb_d.rearrange("(p a) -> p a", a=1))
    pr.sec("gpsimd", fldpe)

    def fcast(e):
        e.tick(e.g.tensor_copy(u_bf[:, :], x_sb[:, 0:nf * T3]))
    pr.sec("gpsimd", fcast)
    mrow, rs = emit_ln(pr, banks, x_sb, nf, T3, ones_bf, u_bf, mode="lnrms",
                       gain_cols=nw_c, name="m2ln", x_bf=u_bf, sq_bf=sqs)

    for kj in range(nf):
        def fh(e, kj=kj):
            e.fence(e.g.tensor_sub(stage_f[:, 0:T], x_sb[:, kj * T3 + 3:(kj + 1) * T3],
                                   mrow[:, 3:T3]))
            e.tick(e.g.tensor_mul(stage_f[:, 0:T], stage_f[:, 0:T], rs[:, 3:T3]))
        pr.sec("vector", fh)

        def fho(e, kj=kj):
            e.dma(h_o[kj * 128:(kj + 1) * 128, :], stage_f[:, 0:T])
        pr.sec("sync", fho)

    ntb3 = cdiv(T3, 512)
    ntb = cdiv(T, 512)
    for mj in range(35):
        is_x = 16 <= mj < 34
        mlen = 32 if mj == 34 else 128
        bw = wstr[mj % 2]
        iw = mj % 2

        def fldw(e, mj=mj, mlen=mlen, bw=bw, iw=iw):
            if wdep[iw] is not None:
                e.need(*wdep[iw])
            else:
                e.waitall("te")
            for kj in range(nf):
                e.dma(bw[:, kj * 128:kj * 128 + mlen],
                      win_d[kj * 128:(kj + 1) * 128, mj * 128:mj * 128 + mlen])
        pr.sec("gpsimd", fldw, auto=False)
        wl = pr.cnt["dpe"]

        nb = ntb3 if is_x else ntb
        for tb in range(nb):
            t0 = tb * 512
            tl = min(512, (T3 if is_x else T) - t0)
            holder = {}

            def ft(e, mj=mj, mlen=mlen, t0=t0, tl=tl, holder=holder, wl=wl, bw=bw, is_x=is_x):
                e.need("dpe", wl)
                b, p = banks.get(e)
                toff = t0 if is_x else t0 + 3
                e.tick(mm(e, p, bw, 128, 0, mlen, u_bf, T3, toff, tl,
                          [(k, k, 128) for k in range(nf)]))
                holder["b"] = (b, p)
            pr.sec("tensor", ft)

            if mj < 16:
                def fs(e, t0=t0, tl=tl, holder=holder):
                    b, p = holder["b"]
                    c = e.tick(e.g.activation(stage_bf[:, t0:t0 + tl], p[:, 0:tl],
                                              AF.Silu))
                    banks.release(b, "se", c)
                pr.sec("scalar", fs)
            elif is_x:
                def fv(e, t0=t0, tl=tl, holder=holder):
                    b, p = holder["b"]
                    c = e.tick(e.g.tensor_copy(chnk[:, t0:t0 + tl], p[:, 0:tl]))
                    banks.release(b, "ve", c)
                pr.sec("vector", fv)
            else:
                def fv(e, t0=t0, tl=tl, holder=holder):
                    b, p = holder["b"]
                    c = e.tick(e.g.tensor_scalar_add(dtacc[0:32, t0:t0 + tl],
                                                     p[0:32, 0:tl], dtb_c[0:32, 0:1]))
                    banks.release(b, "ve", c)
                pr.sec("vector", fv)

        if mj < 16:
            def fso(e, mj=mj):
                e.dma(zs_o[mj * 128:(mj + 1) * 128, :], stage_bf[:, 0:T])
            pr.sec("sync", fso)
        elif is_x:
            ci = mj - 16

            def fcv(e, ci=ci):
                e.fence(e.g.tensor_scalar_mul(cacc[:, 0:T], chnk[:, 0:T],
                                              cw_c[:, ci * 4 + 0:ci * 4 + 1]))
                for tap in range(1, 4):
                    e.fence(e.g.scalar_tensor_tensor(
                        cacc[:, 0:T], chnk[:, tap:tap + T],
                        cw_c[:, ci * 4 + tap:ci * 4 + tap + 1],
                        cacc[:, 0:T], ALU.mult, ALU.add))
            pr.sec("vector", fcv)

            def fsl(e, ci=ci):
                e.tick(e.g.activation(stage_bf[:, 0:T], cacc[:, 0:T], AF.Silu,
                                      bias=cb_c[:, ci:ci + 1]))
            pr.sec("scalar", fsl)

            def fso(e, ci=ci):
                if ci < 16:
                    e.dma(xs_o[ci * 128:(ci + 1) * 128, :], stage_bf[:, 0:T])
                elif ci == 16:
                    e.dma(b_o[:, :], stage_bf[:, 0:T])
                else:
                    e.dma(c_o[:, :], stage_bf[:, 0:T])
            pr.sec("sync", fso)
        else:
            def fsp1(e):
                e.fence(e.g.activation(spa[0:32, 0:T], dtacc[0:32, 0:T], AF.Abs))
                e.fence(e.g.activation(spa[0:32, 0:T], spa[0:32, 0:T], AF.Exp, scale=-1.0))
                e.fence(e.g.activation(spa[0:32, 0:T], spa[0:32, 0:T], AF.Ln, bias=1.0))
            pr.sec("scalar", fsp1)

            def fsp2(e):
                e.fence(e.g.tensor_scalar_max(dtacc[0:32, 0:T], dtacc[0:32, 0:T], 0.0))
                e.tick(e.g.tensor_add(dtacc[0:32, 0:T], dtacc[0:32, 0:T], spa[0:32, 0:T]))
            pr.sec("vector", fsp2)

            def fso(e):
                e.dma(dt_o[0:32, :], dtacc[0:32, 0:T])
            pr.sec("sync", fso)

    def fend(e):
        e.need("dsp")
    pr.sec("sync", fend)
    return pr



def build_m2scan(T=8192, UNITS=2):
    pr = Prog("m2scan")
    xs_d = pr.din("xs_sl", [UNITS * 64, T], BF16)
    dt_d = pr.din("dt_sl", [UNITS, T])
    alog_d = pr.din("alog_sl", [UNITS])
    b_d = pr.din("b_sl", [128, T], BF16)
    c_d = pr.din("c_sl", [128, T], BF16)
    sel_d = pr.din("sel64", [64, 64 * 128], BF16)
    eco_d = pr.din("ecol64", [128, 64 * 64], BF16)
    y_o = pr.dout("y_sl", [UNITS * 64, T], BF16)

    xs = pr.sbuf("xs", [64, T], BF16)
    dxs = pr.sbuf("dxs", [64, T], BF16)
    Bb = pr.sbuf("Bb", [128, T], BF16)
    Cb = pr.sbuf("Cb", [128, T], BF16)
    arep = pr.sbuf("arep", [128, T])
    w_t = pr.sbuf("w_t", [128, T], BF16)
    h_t = pr.sbuf("h_t", [128, T])
    ch = pr.sbuf("ch", [128, 4096], BF16)
    sel = pr.sbuf("sel", [64, 64 * 128], BF16)
    eco = pr.sbuf("eco", [128, 64 * 64], BF16)
    yac = pr.sbuf("yac", [64, T], BF16)
    acol = pr.sbuf("acol", [128, UNITS])
    pA = pr.psum("pA", [128, 2048], F32)
    pB = pr.psum("pB", [128, 2048], F32)

    def fldpe(e):
        e.dma(Bb[:, :], b_d)
        e.dma(Cb[:, :], c_d)
        e.dma(sel[:, :], sel_d)
        e.dma(eco[:, :], eco_d)
        e.dma(acol[0:1, 0:UNITS], alog_d.rearrange("(a u) -> a u", a=1))
    pr.sec("gpsimd", fldpe)

    def fA(e):
        e.fence(e.g.activation(acol[0:1, 0:UNITS], acol[0:1, 0:UNITS], AF.Exp))
    pr.sec("scalar", fA)

    def fA2(e):
        e.fence(e.g.tensor_scalar_mul(acol[0:1, 0:UNITS], acol[0:1, 0:UNITS], -1.0))
    pr.sec("vector", fA2)

    k = 1
    while k < 128:
        def fdbl(e, k=k):
            e.need("dsp")
            e.need("ve")
            e.dma(acol[k:min(2 * k, 128), 0:UNITS], acol[0:min(k, 128 - k), 0:UNITS])
        pr.sec("sync", fdbl, auto=False)
        k *= 2

    for u in range(UNITS):
        def fldu(e, u=u):
            e.dma(arep[0:1, :], dt_d[u:u + 1, :])
            e.dma(xs[:, :], xs_d[u * 64:(u + 1) * 64, :])
        pr.sec("sync", fldu)

        k = 1
        while k < 128:
            def fdbl2(e, k=k):
                e.need("dsp")
                e.dma(arep[k:min(2 * k, 128), :], arep[0:min(k, 128 - k), :])
            pr.sec("sync", fdbl2, auto=False)
            k *= 2

        def fdxs(e, u=u):
            e.tick(e.g.tensor_mul(dxs[:, :], arep[0:64, :], xs[:, :]))
        pr.sec("gpsimd", fdxs)

        def fexp(e, u=u):
            e.tick(e.g.activation(arep[:, :], arep[:, :], AF.Exp,
                                  scale=acol[:, u:u + 1]))
        pr.sec("scalar", fexp)

        def fz(e):
            e.tick(e.g.memset(yac[:, :], 0.0))
        pr.sec("vector", fz)

        ntile = T // 512
        ngrp = cdiv(ntile, 4)
        for p in range(64):
            for g in range(ngrp):
                px = pA if g % 2 == 0 else pB
                gt = min(4, ntile - g * 4)

                def ftb(e, p=p, g=g, px=px, gt=gt):
                    e.need("ve")
                    e.need("se")
                    ins = None
                    for jj in range(gt):
                        tile = g * 4 + jj
                        ins = e.g.matmul(px[:, jj * 512:(jj + 1) * 512],
                                         sel[0:64, p * 128:(p + 1) * 128],
                                         dxs[0:64, tile * 512:(tile + 1) * 512],
                                         start=True, stop=True)
                    e.tick(ins)
                pr.sec("tensor", ftb, auto=False)

                def fw(e, p=p, g=g, px=px, gt=gt):
                    e.need("te")
                    e.tick(e.g.tensor_mul(w_t[:, g * 2048:g * 2048 + gt * 512],
                                          px[:, 0:gt * 512],
                                          Bb[:, g * 2048:g * 2048 + gt * 512]))
                pr.sec("vector", fw, auto=False)

            def fsn(e, p=p):
                e.need("pe")
                e.fence(e.g.tensor_tensor_scan(h_t[:, :], arep[:, :], w_t[:, :], 0.0,
                                               ALU.mult, ALU.add))
            pr.sec("vector", fsn, auto=False)

            nseg = cdiv(ntile, 8)
            for seg in range(nseg):
                st = min(8, ntile - seg * 8)

                def fch(e, p=p, seg=seg, st=st):
                    e.need("te")
                    e.fence(e.g.tensor_mul(ch[:, 0:st * 512],
                                           h_t[:, seg * 4096:seg * 4096 + st * 512],
                                           Cb[:, seg * 4096:seg * 4096 + st * 512]))
                pr.sec("vector", fch, auto=False)

                for g in range(cdiv(st, 4)):
                    px = pA if g == 0 else pB
                    gt = min(4, st - g * 4)

                    def fty(e, p=p, seg=seg, g=g, px=px, gt=gt):
                        e.need("ve")
                        e.need("pe")
                        ins = None
                        for jj in range(gt):
                            tile = g * 4 + jj
                            ins = e.g.matmul(px[0:64, jj * 512:(jj + 1) * 512],
                                             eco[:, p * 64:(p + 1) * 64],
                                             ch[:, tile * 512:(tile + 1) * 512],
                                             start=True, stop=True)
                        e.tick(ins)
                    pr.sec("tensor", fty, auto=False)

                    def fya(e, p=p, seg=seg, g=g, px=px, gt=gt):
                        e.need("te")
                        sp = seg * 4096 + g * 2048
                        e.fence(e.g.tensor_add(yac[:, sp:sp + gt * 512],
                                               yac[:, sp:sp + gt * 512],
                                               px[0:64, 0:gt * 512]))
                    pr.sec("vector", fya, auto=False)

        def fyo(e, u=u):
            e.waitall("ve")
            e.dma(y_o[u * 64:(u + 1) * 64, :], yac[:, :])
        pr.sec("sync", fyo, auto=False)

    def fend(e):
        e.need("dsp")
    pr.sec("sync", fend)
    return pr



def build_m2mid(T=2048):
    pr = Prog("m2mid")
    y_d = pr.din("y", [2048, T], BF16)
    xs_d = pr.din("xs", [2048, T], BF16)
    zs_d = pr.din("zs", [2048, T], BF16)
    dd_d = pr.din("D2048", [2048])
    y2_o = pr.dout("y2", [2048, T], BF16)

    ybuf = pr.sbuf("ybuf", [128, T], BF16)
    xbuf = pr.sbuf("xbuf", [128, T], BF16)
    zbuf = pr.sbuf("zbuf", [128, T], BF16)
    tbuf = pr.sbuf("tbuf", [128, T])
    obuf = pr.sbuf("obuf", [128, T], BF16)
    d_c = pr.sbuf("dc", [128, 16])

    def fldd(e):
        for kj in range(16):
            e.dma(d_c[:, kj:kj + 1], dd_d.rearrange("(k p) -> k p", p=128)[kj].rearrange("(p a) -> p a", a=1))
    pr.sec("gpsimd", fldd)

    for kj in range(16):
        def fld(e, kj=kj):
            e.dma(ybuf[:, :], y_d[kj * 128:(kj + 1) * 128, :])
            e.dma(xbuf[:, :], xs_d[kj * 128:(kj + 1) * 128, :])
            e.dma(zbuf[:, :], zs_d[kj * 128:(kj + 1) * 128, :])
        pr.sec("sync", fld)

        def fv(e, kj=kj):
            e.fence(e.g.scalar_tensor_tensor(tbuf[:, :], xbuf[:, :], d_c[:, kj:kj + 1],
                                             ybuf[:, :], ALU.mult, ALU.add))
            e.tick(e.g.tensor_mul(obuf[:, :], tbuf[:, :], zbuf[:, :]))
        pr.sec("vector", fv)

        def fso(e, kj=kj):
            e.dma(y2_o[kj * 128:(kj + 1) * 128, :], obuf[:, :])
        pr.sec("sync", fso)

    def fend(e):
        e.need("dsp")
    pr.sec("sync", fend)
    return pr



def build_att2(T=8192, QT0=0, QT1=32):
    pr = Prog("att2")
    NQ = QT1 - QT0
    xh_d = pr.din("xh_sl", [64, T], BF16)
    vt_d = pr.din("vt_sl", [T, 64], BF16)
    lt_d = pr.din("lt_sl", [64, 64])
    identbf_d = pr.din("ident_bf", [128, 128], BF16)
    o_d = pr.dout("o_sl", [NQ * 128, 64], BF16)

    banks = Banks(pr, 4)
    xh = pr.sbuf("xh", [64, T], BF16)
    vt = pr.sbuf("vt", [128, (T // 128) * 64], BF16)
    lt = pr.sbuf("lt", [64, 64], BF16)
    wqk = pr.sbuf("wqk", [64, 64], BF16)
    qf = pr.sbuf("qf", [64, NQ * 128], BF16)
    sful = pr.sbuf("sful", [128, T])
    pbf = pr.sbuf("pbf", [128, T], BF16)
    ptb = pr.sbuf("ptb", [128, T], BF16)
    ident_bf = pr.sbuf("identbf", [128, 128], BF16)
    mxa = pr.sbuf("mxa", [128, 24])
    ob = pr.sbuf("ob", [128, 64], BF16)
    po = pr.psum("po", [128, 512], F32)
    pt2 = [pr.psum("pt2", [128, 1024], BF16) for _ in range(2)]

    def fldsp(e):
        e.dma(ident_bf[:, :], identbf_d)
        e.dma(xh[0:64, :], xh_d)
        e.dma(vt[:, :].rearrange("p (n d) -> p n d", d=64),
              vt_d.rearrange("(n p) d -> p n d", p=128))
    pr.sec("sync", fldsp)

    def fldlt(e):
        e.dma(lt[0:64, :], lt_d)
    pr.sec("gpsimd", fldlt)

    holder = {}

    def ftq(e, holder=holder):
        b, p = banks.get(e)
        e.tick(e.g.matmul(p[0:64, 0:64], lt[0:64, :], lt[0:64, :],
                          start=True, stop=True))
        holder["b"] = (b, p)
    pr.sec("tensor", ftq)

    def fvq(e, holder=holder):
        b, p = holder["b"]
        c = e.tick(e.g.tensor_copy(wqk[0:64, :], p[0:64, 0:64]))
        banks.release(b, "ve", c)
    pr.sec("vector", fvq)

    for qb in range(cdiv(NQ * 128, 512)):
        t0 = QT0 * 128 + qb * 512
        tl = min(512, QT1 * 128 - t0)
        holder2 = {}

        def ftq2(e, qb=qb, t0=t0, tl=tl, holder2=holder2):
            b, p = banks.get(e)
            e.tick(e.g.matmul(p[0:64, 0:tl], wqk[0:64, :], xh[0:64, t0:t0 + tl],
                              start=True, stop=True))
            holder2["b"] = (b, p)
        pr.sec("tensor", ftq2)

        def fvq2(e, qb=qb, tl=tl, holder2=holder2):
            b, p = holder2["b"]
            c = e.tick(e.g.tensor_copy(qf[0:64, qb * 512:qb * 512 + tl], p[0:64, 0:tl]))
            banks.release(b, "ve", c)
        pr.sec("vector", fvq2)

    nkb = T // 512
    for qt in range(QT0, QT1):
        qo = (qt - QT0) * 128
        for kg in range(cdiv(nkb, 4)):
            kn = min(4, nkb - kg * 4)
            holder3 = {}

            def fts(e, qt=qt, qo=qo, kg=kg, kn=kn, holder3=holder3):
                bs = []
                for j in range(kn):
                    k0 = (kg * 4 + j) * 512
                    b, p = banks.get(e)
                    ins = e.g.matmul(p[:, 0:512], qf[0:64, qo:qo + 128],
                                     xh[0:64, k0:k0 + 512], start=True, stop=True)
                    if j == kn - 1:
                        e.tick(ins)
                    bs.append((b, p))
                holder3["bs"] = bs
            pr.sec("tensor", fts)

            def fvs(e, kg=kg, kn=kn, holder3=holder3):
                bs = holder3["bs"]
                for j in range(kn):
                    kb = kg * 4 + j
                    b, p = bs[j]
                    e.g.tensor_copy(sful[:, kb * 512:(kb + 1) * 512], p[:, 0:512])
                    c = e.tick(e.g.reduce_max(mxa[:, kb:kb + 1], p[:, 0:512], axis=AX))
                    banks.release(b, "ve", c)
            pr.sec("vector", fvs)

        def fvm(e):
            e.barrier()
            e.fence(e.g.reduce_max(mxa[:, 16:17], mxa[:, 0:nkb], axis=AX))
            e.fence(e.g.tensor_scalar_mul(mxa[:, 16:17], mxa[:, 16:17], -0.125))
        pr.sec("vector", fvm)

        def fse(e):
            e.tick(e.g.activation(pbf[:, 0:T], sful[:, 0:T], AF.Exp,
                                  bias=mxa[:, 16:17], scale=0.125))
        pr.sec("scalar", fse)

        def fvr(e):
            e.fence(e.g.reduce_sum(mxa[:, 17:18], pbf[:, 0:T], axis=AX))
            e.fence(e.g.reciprocal(mxa[:, 17:18], mxa[:, 17:18]))
        pr.sec("vector", fvr)

        for grp in range(cdiv(T // 128, 8)):
            px = pt2[grp % 2]
            gc = min(8, T // 128 - grp * 8)

            def ftt(e, grp=grp, px=px, gc=gc):
                if grp >= 2:
                    e.need("ve")
                ins = None
                for j in range(gc):
                    kc = grp * 8 + j
                    ins = e.g.transpose(px[:, j * 128:(j + 1) * 128],
                                        pbf[:, kc * 128:(kc + 1) * 128],
                                        ident_bf[:, :])
                e.tick(ins)
            pr.sec("tensor", ftt, auto=(grp == 0))

            def fvt(e, grp=grp, px=px, gc=gc):
                e.need("te")
                e.tick(e.g.tensor_copy(ptb[:, grp * 1024:grp * 1024 + gc * 128],
                                       px[:, 0:gc * 128]))
            pr.sec("vector", fvt, auto=False)

        def fta(e):
            e.need("ve")
            ins = None
            for kc in range(T // 128):
                ins = e.g.matmul(po[:, 0:64], ptb[:, kc * 128:(kc + 1) * 128],
                                 vt[:, kc * 64:(kc + 1) * 64],
                                 start=(kc == 0), stop=(kc == T // 128 - 1))
            e.tick(ins)
        pr.sec("tensor", fta, auto=False)

        def fvo(e):
            e.tick(e.g.tensor_scalar_mul(ob[:, 0:64], po[:, 0:64], mxa[:, 17:18]))
        pr.sec("vector", fvo)

        def fsoo(e, qt=qt, qo=qo):
            e.dma(o_d[qo:qo + 128, :], ob[:, 0:64])
        pr.sec("sync", fsoo)

    def fend(e):
        e.need("dsp")
    pr.sec("sync", fend)
    return pr



def build_lnf(T=2048):
    pr = Prog("lnf")
    nf = 8
    x_d = pr.din("x", [D_MODEL, T])
    ones_d = pr.din("ones_bf", [128, 128], BF16)
    h_o = pr.dout("h", [D_MODEL, T])

    banks = Banks(pr, 4)
    x_sb = pr.sbuf("x", [128, nf * T])
    x_bf = pr.sbuf("xb", [128, nf * T], BF16)
    sqs = pr.sbuf("sqs", [128, nf * T], BF16)
    ones_bf = pr.sbuf("ones", [128, 128], BF16)
    stage = pr.sbuf("stage", [128, T])

    def fldsp(e):
        for kj in range(nf):
            e.dma(x_sb[:, kj * T:(kj + 1) * T], x_d[kj * 128:(kj + 1) * 128, :])
        e.dma(ones_bf[:, :], ones_d)
    pr.sec("sync", fldsp)

    def fcast(e):
        e.tick(e.g.tensor_copy(x_bf[:, :], x_sb[:, 0:nf * T]))
    pr.sec("gpsimd", fcast)
    mrow, rs = emit_ln(pr, banks, x_sb, nf, T, ones_bf, x_bf, mode="ln",
                       name="lnf", x_bf=x_bf, sq_bf=sqs, apply=False)

    for kj in range(nf):
        def fh(e, kj=kj):
            e.fence(e.g.tensor_sub(stage[:, 0:T], x_sb[:, kj * T:(kj + 1) * T],
                                   mrow[:, 0:T]))
            e.tick(e.g.tensor_mul(stage[:, 0:T], stage[:, 0:T], rs[:, 0:T]))
        pr.sec("vector", fh)

        def fho(e, kj=kj):
            e.dma(h_o[kj * 128:(kj + 1) * 128, :], stage[:, 0:T])
        pr.sec("sync", fho)

    def fend(e):
        e.need("dsp")
    pr.sec("sync", fend)
    return pr



_PROGS = {}
TRACE = None


def _trace(name, arr):
    if TRACE is not None:
        TRACE[name] = np.array(arr, copy=True)


def _prog(name, builder, **kw):
    if name not in _PROGS:
        _PROGS[name] = builder(**kw).close()
    return _PROGS[name]


TQ = 2048


def _tsl(arr, c):
    b, q = c // 4, c % 4
    return np.ascontiguousarray(arr[b][:, q * TQ:(q + 1) * TQ])


def _thalo(arr, c, halo):
    b, q = c // 4, c % 4
    F = arr.shape[1]
    out = np.zeros((F, TQ + halo), arr.dtype)
    t0 = q * TQ
    if q == 0:
        out[:, halo:] = arr[b][:, 0:TQ]
    else:
        out[:, :] = arr[b][:, t0 - halo:t0 + TQ]
    return np.ascontiguousarray(out)


def _asm(res, key, F, dtype=np.float32):
    out = np.empty((2, F, 8192), dtype)
    for c in range(8):
        b, q = c // 4, c % 4
        out[b][:, q * TQ:(q + 1) * TQ] = np.asarray(res.results[c][key])
    return out


def kernel(**inputs):
    p = prep_inputs(inputs)
    run = run_bass_kernel_spmd
    cores = list(range(8))
    x0 = p["x_fm"]

    nc = _prog("m2pre", build_m2pre)
    maps = [{"x": _thalo(x0, c, 3), "ones_bf": p["ones_bf"], "norm_w": p["l0_norm_w"],
             "w_in": p["w_inproj"], "conv_w": p["l0_conv_w"], "conv_b": p["l0_conv_b"],
             "dt_bias": p["l0_dt_bias"]} for c in range(8)]
    r = run(nc, maps, core_ids=cores)
    h_all = _asm(r, "h", 1024)
    zs_all = _asm(r, "zs", 2048, BF)
    xs_all = _asm(r, "xs", 2048, BF)
    B_all = _asm(r, "bm", 128, BF)
    C_all = _asm(r, "cm", 128, BF)
    dt_all = _asm(r, "dt", 32)

    nc = _prog("m2scan", build_m2scan)
    y_all = np.empty((2, 2048, 8192), BF)
    for L in range(4):
        maps = []
        for c in range(8):
            b = c // 4
            h0 = (c % 4) * 8 + L * 2
            maps.append({"xs_sl": np.ascontiguousarray(xs_all[b][h0 * 64:h0 * 64 + 128]),
                         "dt_sl": np.ascontiguousarray(dt_all[b][h0:h0 + 2]),
                         "alog_sl": np.ascontiguousarray(p["l0_A_log"][h0:h0 + 2]),
                         "b_sl": B_all[b], "c_sl": C_all[b],
                         "sel64": p["sel64"], "ecol64": p["ecol64"]})
        r = run(nc, maps, core_ids=cores)
        for c in range(8):
            b = c // 4
            h0 = (c % 4) * 8 + L * 2
            y_all[b][h0 * 64:h0 * 64 + 128] = np.asarray(r.results[c]["y_sl"])

    nc = _prog("m2mid", build_m2mid)
    maps = [{"y": _tsl(y_all, c), "xs": _tsl(xs_all, c), "zs": _tsl(zs_all, c),
             "D2048": p["l0_D2048"]} for c in range(8)]
    r = run(nc, maps, core_ids=cores)
    y2_all = _asm(r, "y2", 2048, BF)

    nc = _prog("l0post1", build_l0post1)
    maps = [{"y2": _tsl(y2_all, c), "ones_bf": p["ones_bf"], "ssm_w": p["l0_ssm_norm_w"],
             "w_out": p["w_outproj"]} for c in range(8)]
    r = run(nc, maps, core_ids=cores)
    mix_all = _asm(r, "mix", 1024)

    nc = _prog("l0post2", build_l0post2)
    maps = [{"x_outer": _tsl(x0, c), "hid": _tsl(h_all, c), "mix": _tsl(mix_all, c),
             "ones_bf": p["ones_bf"], "post_w": p["l0_post_norm_w"],
             "w_gate": p["w_gate"], "w_up": p["w_up"], "w_down": p["w_down"]}
            for c in range(8)]
    r = run(nc, maps, core_ids=cores)
    x_run = _asm(r, "xo", 1024)
    _trace("x1", x_run)

    def moe15_step(xin):
        nc = _prog("moe15", build_moe15)
        com = {"ones_bf": p["ones_bf"], "ident_f32": p["ident_f32"],
               "ident_bf": p["ident_bf"], "bcast15": p["bcast15"],
               "rW": p["m15_router"], "rb": p["m15_router_b"],
               "sW1": p["m15_sW1"], "sb1": p["m15_sb1"],
               "sW2": p["m15_sW2"], "sb2": p["m15_sb2"],
               "eW1": p["m15_eW1"], "eb1": p["m15_eb1"],
               "eW2": p["m15_eW2"], "eb2": p["m15_eb2"]}
        maps = [dict(com, x=_tsl(xin, c)) for c in range(8)]
        r = run(nc, maps, core_ids=cores)
        return _asm(r, "xo", 1024)

    x_run = moe15_step(x_run)
    _trace("x2", x_run)

    nc = _prog("s4a", build_s4a)
    iota = np.arange(8192, dtype=np.float32).reshape(1, 8192)
    ls16 = np.repeat(p["s4_log_step"], 16).astype(np.float32)
    maps = []
    for c in range(8):
        b, nq = c // 4, c % 4
        sl = slice(nq * 16, (nq + 1) * 16)
        maps.append({"x": x_run[b], "ones_bf": p["ones_bf"], "iota_row": iota,
                     "BT_sl": np.ascontiguousarray(p["s4_BT"][:, sl]),
                     "lar_sl": np.ascontiguousarray(p["s4_lAr"][sl]),
                     "ai_sl": np.ascontiguousarray(p["s4_Ai"][sl]),
                     "log_step16": ls16})
    r = run(nc, maps, core_ids=cores)
    xsr_all = np.empty((2, 64, 8192), np.float32)
    xsi_all = np.empty((2, 64, 8192), np.float32)
    for c in range(8):
        b, nq = c // 4, c % 4
        xsr_all[b][nq * 16:(nq + 1) * 16] = np.asarray(r.results[c]["xsr"])
        xsi_all[b][nq * 16:(nq + 1) * 16] = np.asarray(r.results[c]["xsi"])

    nc = _prog("s4b", build_s4b)
    maps = [{"x": _tsl(x_run, c), "xsr": _tsl(xsr_all, c), "xsi": _tsl(xsi_all, c),
             "ones_bf": p["ones_bf"], "CrT": p["s4_CrT"], "CiT": p["s4_CiT"],
             "Dv": p["s4_D"]} for c in range(8)]
    r = run(nc, maps, core_ids=cores)
    x_run = _asm(r, "xo", 1024)
    _trace("x3", x_run)

    x4_all = moe15_step(x_run)
    _trace("x4", x4_all)

    nc = _prog("lnf", build_lnf)
    maps = [{"x": _tsl(x4_all, c), "ones_bf": p["ones_bf"]} for c in range(8)]
    r = run(nc, maps, core_ids=cores)
    xp_run = _asm(r, "h", 1024)

    for li in range(8):
        nc = _prog("m1pre", build_m1pre)
        maps = [{"xp": _thalo(xp_run, c, 4), "ones_bf": p["ones_bf"],
                 "w_in": p["m1_Win"][li], "conv_w": p["m1_conv_w"][li],
                 "conv_b": p["m1_conv_b"][li], "w_x": p["m1_Wx"][li],
                 "w_dt": p["m1_Wdt"][li], "b_dt": p["m1_bdt"][li]}
                for c in range(8)]
        r = run(nc, maps, core_ids=cores)
        xin_all = _asm(r, "xin", 2048, BF)
        zs1_all = _asm(r, "zs", 2048, BF)
        del_all = _asm(r, "delta", 2048)
        bc_all = _asm(r, "bc", 32)

        nc = _prog("m1scan", build_m1scan)
        y1_all = np.empty((2, 2048, 8192), np.float32)
        for b in range(2):
            maps = [{"delta_sl": np.ascontiguousarray(del_all[b][c * 256:(c + 1) * 256]),
                     "xin_sl": np.ascontiguousarray(xin_all[b][c * 256:(c + 1) * 256]),
                     "bc": bc_all[b],
                     "alog_sl": np.ascontiguousarray(p["m1_Alog"][li][c * 256:(c + 1) * 256]),
                     "bcast16": p["bcast16"]} for c in range(8)]
            r = run(nc, maps, core_ids=cores)
            for c in range(8):
                y1_all[b][c * 256:(c + 1) * 256] = np.asarray(r.results[c]["y_sl"])

        last = li == 7
        nc = _prog("m1post_last" if last else "m1post", build_m1post, last=last)
        maps = []
        for c in range(8):
            m = {"xp": _tsl(xp_run, c), "y": _tsl(y1_all, c), "xin": _tsl(xin_all, c),
                 "zs": _tsl(zs1_all, c), "dp": p["m1_D"][li], "w_out": p["m1_Wout"][li]}
            if last:
                m["x_outer"] = _tsl(x4_all, c)
            maps.append(m)
        r = run(nc, maps, core_ids=cores)
        xp_run = _asm(r, "xo", 1024)
        _trace(f"m1_{li}", xp_run)
    x_run = xp_run
    _trace("x5", x_run)

    def moe32_step(xin):
        nc = _prog("moe32", build_moe32)
        com = {"ones_bf": p["ones_bf"], "ident_bf": p["ident_bf"],
               "bcast32": p["bcast32"], "rW": p["m32_router"], "rb": p["m32_router_b"],
               "gate": p["m32_gate"], "up": p["m32_up"], "down": p["m32_down"],
               "sgate": p["m32_sgate"], "sup": p["m32_sup"], "sdown": p["m32_sdown"]}
        maps = [dict(com, x=_tsl(xin, c)) for c in range(8)]
        r = run(nc, maps, core_ids=cores)
        return _asm(r, "xo", 1024)

    x_run = moe32_step(x_run)
    x6_all = x_run
    _trace("x6", x_run)

    nc = _prog("attpre", build_attpre)
    maps = [{"x": _tsl(x6_all, c), "ones_bf": p["ones_bf"], "wv": p["att_Wv"],
             "bv": p["att_bv"]} for c in range(8)]
    r = run(nc, maps, core_ids=cores)
    ln_all = _asm(r, "ln_bf", 1024, BF)
    vh_all = _asm(r, "vh_bf", 1024, BF)
    v_tm = [np.ascontiguousarray(vh_all[b].T) for b in range(2)]

    o_tm = [np.empty((8192, 1024), BF) for _ in range(2)]
    for qhalf in range(2):
        nc = _prog(f"att2_{qhalf}", build_att2, QT0=qhalf * 32, QT1=(qhalf + 1) * 32)
        for g in range(4):
            maps = []
            for c in range(8):
                unit = g * 8 + c
                b, hh = unit // 16, unit % 16
                maps.append({"xh_sl": np.ascontiguousarray(ln_all[b][hh * 64:(hh + 1) * 64]),
                             "vt_sl": np.ascontiguousarray(v_tm[b][:, hh * 64:(hh + 1) * 64]),
                             "lt_sl": np.ascontiguousarray(p["att_L"][hh].T),
                             "ident_bf": p["ident_bf"]})
            r = run(nc, maps, core_ids=cores)
            for c in range(8):
                unit = g * 8 + c
                b, hh = unit // 16, unit % 16
                o_tm[b][qhalf * 4096:(qhalf + 1) * 4096, hh * 64:(hh + 1) * 64] = \
                    np.asarray(r.results[c]["o_sl"])
    o_fm = np.stack([np.ascontiguousarray(o_tm[b].T) for b in range(2)])

    nc = _prog("attpost", build_attpost)
    maps = [{"x": _tsl(x6_all, c), "o_fm": _tsl(o_fm, c), "wo": p["att_Wo"],
             "bo": p["att_bo"]} for c in range(8)]
    r = run(nc, maps, core_ids=cores)
    x_run = _asm(r, "xo", 1024)
    _trace("x7", x_run)

    x_run = moe32_step(x_run)

    return np.ascontiguousarray(x_run.transpose(0, 2, 1))
